# version 05 (final)
# baseline (speedup 1.0000x reference)
import numpy as np
from contextlib import ExitStack

import concourse.bass as bass
import concourse.mybir as mybir
from concourse.bass_utils import run_bass_kernel_spmd

F32 = mybir.dt.float32
BF16 = mybir.dt.float16
AB16 = mybir.dt.bfloat16
AF = mybir.ActivationFunctionType
ALU = mybir.AluOpType
AX = mybir.AxisListType
ET = mybir.EngineType

D = 1024
T = 4096
TT = 512
NCH = D // 128
NTILE = T // TT
FF = 4096
NFF = FF // 128
NB = 8
NHEAD = 16
HDIM = 64
LN8 = 2.0794415416798357


class G:
    _uid = 0

    def u(self, name):
        G._uid += 1
        return f"{name}_{G._uid}"


def _fm(hbm_ap, p=128):
    return hbm_ap.rearrange("(c p) n -> p c n", p=p)


U8 = mybir.dt.uint8


def _fmu(hbm_ap, p=128):
    return hbm_ap.bitcast(U8).rearrange("(c p) n -> p c n", p=p)


from concourse.ordered_engine_set import OrderedEngineSet
from contextlib import contextmanager


@contextmanager
def fori_clean(nc, n, engines):
    loop_name = f"fc_{nc.next_id()}"
    engines = OrderedEngineSet(engines)
    registers = nc.alloc_registers(f"{loop_name}_i", engines=engines)
    nc.regs_mov(registers, 0)
    nc.br(loop_name + "_loop", engines=engines)
    with nc.body(loop_name + "_loop", valid_engines=engines):
        yield None
        nc.regs_alu(registers, registers, 1, op=ALU.add)
        nc.br_lt(registers, n, on_true=loop_name + "_loop",
                 on_false=loop_name + "_end", engines=engines)
    nc.switch_bb(loop_name + "_end")
    for h in registers.handles:
        nc.engines[h.engine].free_register(h)


class R:

    def __init__(self, g, eng, init, hi=1 << 30):
        self.eng = eng
        self.h = eng.alloc_register(g.u("r"))
        eng.reg_mov(self.h, init)
        self._v = None
        self._hi = hi

    @property
    def v(self):
        if self._v is None:
            self._v = self.eng.snap(self.h, donate=True, min_val=0, max_val=self._hi)
        return self._v

    def bump(self, d):
        if d:
            self.eng.reg_add(self.h, self.h, d)

    def free(self):
        self.eng.free_register(self.h)


class RegSet:
    def __init__(self, g):
        self.g = g
        self.regs = []

    def r(self, eng, init, hi=1 << 30):
        x = R(self.g, eng, init, hi)
        self.regs.append(x)
        return x

    def close(self):
        for x in self.regs:
            x.free()


def stage_setup(g):
    nc = g.nc
    with nc.reset_on_exit():
        with ExitStack() as st:
            sL = st.enter_context(nc.semaphore("sL"))
            sP = st.enter_context(nc.semaphore("sP"))
            sA = st.enter_context(nc.semaphore("sA"))
            sV = st.enter_context(nc.semaphore("sV"))
            vrow = st.enter_context(nc.sbuf_tensor(g.u("vrow"), [1, 2 * 4096], F32))
            pts = [st.enter_context(nc.psum_tensor(g.u(f"pt{i}"), [128, 32], F32)) for i in range(2)]

            nc.vector.memset(g.ones_col_bf[:], 1.0)
            nc.vector.memset(g.onesf[:], 1.0).then_inc(sV, 1)
            nc.tensor.wait_ge(sV, 1)

            nl, npe, na = [0], [0], [0]

            def load_cols(vec_rows_ap, dest_col_ap, nrow):
                i = na[0]
                sl = (i % 2) * 4096
                flat = vec_rows_ap.rearrange("r c -> (r c)")
                nc.sync.wait_ge(sA, max(0, i - 1))
                nc.sync.dma_start(out=vrow[0:1, sl:sl + nrow * 128],
                                  in_=flat[None, :]).then_inc(sL, 16)
                nl[0] += 1
                nc.tensor.wait_ge(sL, 16 * nl[0])
                nc.tensor.wait_ge(sA, i)
                for j in range(nrow):
                    ins = nc.tensor.matmul(pts[i % 2][:, j:j + 1],
                                           vrow[0:1, sl + j * 128:sl + (j + 1) * 128],
                                           g.onesf[0:1, 0:1], start=True, stop=True)
                ins.then_inc(sP, 1)
                npe[0] += 1
                nc.scalar.wait_ge(sP, npe[0])
                nc.scalar.copy(dest_col_ap, pts[i % 2][:, 0:nrow]).then_inc(sA, 1)
                na[0] += 1

            for (vec, dest, nrow) in g.setup_cols:
                load_cols(vec, dest, nrow)
            nc.vector.wait_ge(sA, na[0])
            nc.vector.tensor_scalar_mul(g.nbq_c[:], g.bq_c[:], -1.0)
            nc.vector.tensor_scalar_mul(g.nbk_c[:], g.bk_c[:], -1.0).then_inc(sP, 1)
            nc.sync.wait_ge(sP, npe[0] + 1)
            nc.sync.wait_ge(sA, na[0])


def stage_stats(g, src_hbm, tag):
    nc = g.nc
    srcu = _fmu(src_hbm)
    TTB = TT * 4
    with nc.reset_on_exit():
        with ExitStack() as st:
            xt = st.enter_context(nc.sbuf_tensor(g.u(f"xt{tag}"), [128, 2 * NCH * TT], F32))
            xb = st.enter_context(nc.sbuf_tensor(g.u(f"xb{tag}"), [128, 2 * NCH * TT], BF16))
            xq = st.enter_context(nc.sbuf_tensor(g.u(f"xq{tag}"), [128, 2 * NCH * TT], BF16))
            srow = st.enter_context(nc.sbuf_tensor(g.u(f"sr{tag}"), [1, 4 * TT], F32))
            ps = [st.enter_context(nc.psum_tensor(g.u(f"ps{i}{tag}"), [1, TT], F32)) for i in range(4)]
            sL = st.enter_context(nc.semaphore("sL"))
            sA = st.enter_context(nc.semaphore("sA"))
            sP = st.enter_context(nc.semaphore("sP"))
            sV = st.enter_context(nc.semaphore("sV"))
            sS = st.enter_context(nc.semaphore("sS"))
            rs = RegSet(g)
            nc.scalar.sem_inc(sA, 2)
            nc.tensor.sem_inc(sP, 2)
            nc.vector.sem_inc(sV, 2)
            nc.sync.sem_inc(sS, 16 * 4)
            sp_wa = rs.r(nc.sync, 1)
            sp_wv = rs.r(nc.sync, 3)
            sp_off = rs.r(nc.sync, 0, hi=(NTILE - 1) * TTB)
            sp_os1 = rs.r(nc.sync, 0, hi=(NTILE - 1) * TTB)
            sp_os2 = rs.r(nc.sync, 4 * T, hi=4 * T + (NTILE - 1) * TTB)
            sc_wl = rs.r(nc.scalar, 16)
            sc_wp = rs.r(nc.scalar, 1)
            te_wa = rs.r(nc.tensor, 3)
            te_wv = rs.r(nc.tensor, 1)
            ve_wp = rs.r(nc.vector, 3)
            ve_ws = rs.r(nc.vector, 32)
            stat_u = g.statraw.bitcast(U8)
            with fori_clean(nc, NTILE // 2, [ET.SP, ET.Activation, ET.PE, ET.DVE]):
                for par in range(2):
                    o = par * NCH * TT
                    xtv, xbv, xqv = xt[:, o:o + NCH * TT], xb[:, o:o + NCH * TT], xq[:, o:o + NCH * TT]
                    nc.sync.wait_ge(sA, sp_wa.h)
                    nc.sync.dma_start(out=xtv.bitcast(U8),
                                      in_=srcu[:, :, bass.DynSlice(sp_off.v, TTB)]).then_inc(sL, 16)
                    nc.scalar.wait_ge(sL, sc_wl.h)
                    nc.scalar.wait_ge(sP, sc_wp.h)
                    for m in range(NCH):
                        nc.scalar.activation(xbv[:, m * TT:(m + 1) * TT], xtv[:, m * TT:(m + 1) * TT], AF.Copy)
                        ins = nc.scalar.activation(xqv[:, m * TT:(m + 1) * TT], xtv[:, m * TT:(m + 1) * TT], AF.Square)
                    ins.then_inc(sA, 1)
                    pS, pQ = ps[2 * par], ps[2 * par + 1]
                    nc.tensor.wait_ge(sA, te_wa.h)
                    nc.tensor.wait_ge(sV, te_wv.h)
                    for m in range(NCH):
                        nc.tensor.matmul(pS[:], g.ones_col_bf[:], xbv[:, m * TT:(m + 1) * TT],
                                         start=(m == 0), stop=(m == NCH - 1))
                    for m in range(NCH):
                        ins = nc.tensor.matmul(pQ[:], g.ones_col_bf[:], xqv[:, m * TT:(m + 1) * TT],
                                               start=(m == 0), stop=(m == NCH - 1))
                    ins.then_inc(sP, 1)
                    nc.vector.wait_ge(sP, ve_wp.h)
                    nc.vector.wait_ge(sS, ve_ws.h)
                    nc.vector.tensor_copy(srow[0:1, par * TT:(par + 1) * TT], pS[:])
                    nc.vector.tensor_copy(srow[0:1, (2 + par) * TT:(3 + par) * TT], pQ[:]).then_inc(sV, 1)
                    nc.sync.wait_ge(sV, sp_wv.h)
                    nc.sync.dma_start(out=stat_u[0:1, bass.DynSlice(sp_os1.v, TTB)],
                                      in_=srow[0:1, par * TT:(par + 1) * TT].bitcast(U8)).then_inc(sS, 16)
                    nc.sync.dma_start(out=stat_u[0:1, bass.DynSlice(sp_os2.v, TTB)],
                                      in_=srow[0:1, (2 + par) * TT:(3 + par) * TT].bitcast(U8)).then_inc(sS, 16)
                    sp_wa.bump(1); sp_wv.bump(1); sp_off.bump(TTB); sp_os1.bump(TTB); sp_os2.bump(TTB)
                    sc_wl.bump(16); sc_wp.bump(1)
                    te_wa.bump(1); te_wv.bump(1)
                    ve_wp.bump(1); ve_ws.bump(32)
            nc.sync.wait_ge(sS, 16 * (4 + 2 * NTILE))
            rs.close()


def stage_ln_finalize(g, eps, n_feat=D):
    nc = g.nc
    with nc.reset_on_exit():
        with ExitStack() as st:
            sr = st.enter_context(nc.sbuf_tensor(g.u("fsr"), [1, 2 * T], F32))
            m = st.enter_context(nc.sbuf_tensor(g.u("fme"), [1, T], F32))
            v = st.enter_context(nc.sbuf_tensor(g.u("fva"), [1, T], F32))
            a = st.enter_context(nc.sbuf_tensor(g.u("fab"), [1, 2 * T], F32))
            sL = st.enter_context(nc.semaphore("sL"))
            sA = st.enter_context(nc.semaphore("sA"))
            sV = st.enter_context(nc.semaphore("sV"))
            sS = st.enter_context(nc.semaphore("sS"))
            nc.sync.dma_start(out=sr[:], in_=g.statraw[:]).then_inc(sL, 16)
            nc.scalar.wait_ge(sL, 16)
            nc.scalar.activation(m[:], sr[0:1, 0:T], AF.Copy, scale=1.0 / n_feat)
            nc.scalar.activation(v[:], sr[0:1, T:2 * T], AF.Copy, scale=1.0 / n_feat)
            nc.scalar.activation(sr[0:1, 0:T], m[:], AF.Square).then_inc(sA, 1)
            nc.vector.wait_ge(sA, 1)
            nc.vector.scalar_tensor_tensor(v[:], v[:], float(eps), sr[0:1, 0:T],
                                           ALU.add, ALU.subtract).then_inc(sV, 1)
            nc.scalar.wait_ge(sV, 1)
            nc.scalar.activation(v[:], v[:], AF.Ln)
            nc.scalar.activation(a[0:1, 0:T], v[:], AF.Exp, scale=-0.5).then_inc(sA, 1)
            nc.vector.wait_ge(sA, 2)
            nc.vector.tensor_mul(a[0:1, T:2 * T], m[:], a[0:1, 0:T])
            nc.vector.tensor_scalar_mul(a[0:1, T:2 * T], a[0:1, T:2 * T], -1.0).then_inc(sV, 2)
            nc.scalar.wait_ge(sV, 2)
            nc.scalar.dma_start(out=g.rowAB[:], in_=a[:]).then_inc(sS, 16)
            nc.scalar.wait_ge(sS, 16)


def stage_apply(g, src_hbm, gcol, bcol, out_bf, out_f32, res_hbm=None, tag="ap"):
    nc = g.nc
    srcu = _fmu(src_hbm)
    outbu = _fmu(out_bf) if out_bf is not None else None
    outfu = _fmu(out_f32) if out_f32 is not None else None
    resu = _fmu(res_hbm) if res_hbm is not None else None
    n_out = (outbu is not None) + (outfu is not None)
    n_ld = 3 + (resu is not None)
    TTB = TT * 4
    with nc.reset_on_exit():
        with ExitStack() as st:
            xt = st.enter_context(nc.sbuf_tensor(g.u(f"xt{tag}"), [128, 2 * NCH * TT], F32))
            rt = st.enter_context(nc.sbuf_tensor(g.u(f"rt{tag}"), [128, 2 * NCH * TT], F32)) if resu is not None else None
            abt = st.enter_context(nc.sbuf_tensor(g.u(f"ab{tag}"), [1, 4 * TT], F32))
            yb = st.enter_context(nc.sbuf_tensor(g.u(f"yb{tag}"), [128, 2 * NCH * TT], BF16)) if outbu is not None else None
            yf = st.enter_context(nc.sbuf_tensor(g.u(f"yf{tag}"), [128, 2 * NCH * TT], F32)) if outfu is not None else None
            aux = st.enter_context(nc.sbuf_tensor(g.u(f"ax{tag}"), [128, 2 * NCH * TT], F32))
            pA = [st.enter_context(nc.psum_tensor(g.u(f"pA{i}{tag}"), [128, TT], F32)) for i in range(2)]
            pB = [st.enter_context(nc.psum_tensor(g.u(f"pB{i}{tag}"), [128, TT], F32)) for i in range(2)]
            sL = st.enter_context(nc.semaphore("sL"))
            sP = st.enter_context(nc.semaphore("sP"))
            sV = st.enter_context(nc.semaphore("sV"))
            sS = st.enter_context(nc.semaphore("sS"))
            rs = RegSet(g)
            nc.tensor.sem_inc(sP, 2)
            nc.vector.sem_inc(sV, 6)
            nc.sync.sem_inc(sS, 16 * 2 * n_out)
            sp_wv1 = rs.r(nc.sync, 1)
            sp_wv2 = rs.r(nc.sync, 9)
            sp_off = rs.r(nc.sync, 0, hi=(NTILE - 1) * TTB)
            sp_offr = rs.r(nc.sync, 0, hi=(NTILE - 1) * TTB) if resu is not None else None
            sp_offa = rs.r(nc.sync, 0, hi=(NTILE - 1) * TTB)
            sp_offb = rs.r(nc.sync, 4 * T, hi=4 * T + (NTILE - 1) * TTB)
            sp_offob = rs.r(nc.sync, 0, hi=(NTILE - 1) * TT * 2) if outbu is not None else None
            sp_offof = rs.r(nc.sync, 0, hi=(NTILE - 1) * TTB) if outfu is not None else None
            te_wl = rs.r(nc.tensor, 16 * n_ld)
            te_wv = rs.r(nc.tensor, 1)
            ve_wl1 = rs.r(nc.vector, 16 * 1)
            ve_wl2 = rs.r(nc.vector, 16 * n_ld)
            ve_wp = rs.r(nc.vector, 3)
            ve_ws = rs.r(nc.vector, 16 * n_out)
            ab_u = g.rowAB.bitcast(U8)
            with fori_clean(nc, NTILE // 2, [ET.SP, ET.PE, ET.DVE]):
                for par in range(2):
                    o = par * NCH * TT
                    xtv = xt[:, o:o + NCH * TT]
                    nc.sync.wait_ge(sV, sp_wv1.h)
                    nc.sync.dma_start(out=xtv.bitcast(U8),
                                      in_=srcu[:, :, bass.DynSlice(sp_off.v, TTB)]).then_inc(sL, 16)
                    nc.sync.dma_start(out=abt[0:1, par * 2 * TT:par * 2 * TT + TT].bitcast(U8),
                                      in_=ab_u[0:1, bass.DynSlice(sp_offa.v, TTB)]).then_inc(sL, 16)
                    nc.sync.dma_start(out=abt[0:1, par * 2 * TT + TT:par * 2 * TT + 2 * TT].bitcast(U8),
                                      in_=ab_u[0:1, bass.DynSlice(sp_offb.v, TTB)]).then_inc(sL, 16)
                    if resu is not None:
                        nc.sync.dma_start(out=rt[:, o:o + NCH * TT].bitcast(U8),
                                          in_=resu[:, :, bass.DynSlice(sp_offr.v, TTB)]).then_inc(sL, 16)
                    nc.tensor.wait_ge(sL, te_wl.h)
                    nc.tensor.wait_ge(sV, te_wv.h)
                    nc.tensor.matmul(pA[par][:], g.onesf[:], abt[0:1, par * 2 * TT:par * 2 * TT + TT],
                                     start=True, stop=True)
                    nc.tensor.matmul(pB[par][:], g.onesf[:], abt[0:1, par * 2 * TT + TT:par * 2 * TT + 2 * TT],
                                     start=True, stop=True).then_inc(sP, 1)
                    nc.vector.wait_ge(sL, ve_wl1.h)
                    nc.vector.wait_ge(sP, ve_wp.h)
                    av = aux[:, o:o + NCH * TT]
                    for m in range(NCH):
                        nc.vector.tensor_mul(av[:, m * TT:(m + 1) * TT], xtv[:, m * TT:(m + 1) * TT], pA[par][:])
                        ins = nc.vector.tensor_add(av[:, m * TT:(m + 1) * TT], av[:, m * TT:(m + 1) * TT], pB[par][:])
                    ins.then_inc(sV, 1)
                    nc.vector.wait_ge(sS, ve_ws.h)
                    for m in range(NCH):
                        s0, s1 = m * TT, (m + 1) * TT
                        for dst in (([yb] if outbu is not None else []) + ([yf] if outfu is not None else [])):
                            if gcol is not None:
                                ins = nc.vector.tensor_scalar(dst[:, o + s0:o + s1], av[:, s0:s1],
                                                              gcol[:, m:m + 1], bcol[:, m:m + 1],
                                                              ALU.mult, ALU.add)
                            else:
                                ins = nc.vector.tensor_copy(dst[:, o + s0:o + s1], av[:, s0:s1])
                    ins.then_inc(sV, 1)
                    if resu is not None:
                        nc.vector.wait_ge(sL, ve_wl2.h)
                        for m in range(NCH):
                            s0 = o + m * TT
                            ins = nc.vector.tensor_add(yf[:, s0:s0 + TT], yf[:, s0:s0 + TT],
                                                       rt[:, s0:s0 + TT])
                        ins.then_inc(sV, 1)
                    else:
                        nc.vector.engine_nop().then_inc(sV, 1)
                    nc.sync.wait_ge(sV, sp_wv2.h)
                    if outbu is not None:
                        nc.sync.dma_start(out=outbu[:, :, bass.DynSlice(sp_offob.v, TT * 2)],
                                          in_=yb[:, o:o + NCH * TT].bitcast(U8)).then_inc(sS, 16)
                    if outfu is not None:
                        nc.sync.dma_start(out=outfu[:, :, bass.DynSlice(sp_offof.v, TTB)],
                                          in_=yf[:, o:o + NCH * TT].bitcast(U8)).then_inc(sS, 16)
                    sp_wv1.bump(3); sp_wv2.bump(3); sp_off.bump(TTB)
                    sp_offa.bump(TTB); sp_offb.bump(TTB)
                    if sp_offr is not None: sp_offr.bump(TTB)
                    if sp_offob is not None: sp_offob.bump(TT * 2)
                    if sp_offof is not None: sp_offof.bump(TTB)
                    nc.tensor.wait_ge(sL, 0)
                    te_wl.bump(16 * n_ld); te_wv.bump(3)
                    ve_wl1.bump(16 * n_ld); ve_wl2.bump(16 * n_ld); ve_wp.bump(1); ve_ws.bump(16 * n_out)
            nc.sync.wait_ge(sS, 16 * (2 * n_out + n_out * NTILE))
            rs.close()


def build_program(n_stage_limit=None, debug_taps=()):
    nc = bass.Bass("TRN2", target_bir_lowering=False)
    g = G()
    g.nc = nc

    def inp(name, shape):
        return nc.dram_tensor(name, list(shape), F32, kind="ExternalInput").ap()

    g.in_x = inp("x_fm", [D, T])
    g.in_w_qkv = inp("w_qkv", [D, 3 * D])
    g.in_b_qkv = inp("b_qkv", [24, 128])
    g.in_w_out = inp("w_out", [D, D])
    g.in_b_out = inp("b_out", [8, 128])
    g.in_gam = inp("ln_gamma", [8, 128])
    g.in_bet = inp("ln_beta", [8, 128])
    for L in ("f0", "f1"):
        setattr(g, f"in_{L}_w1T", inp(f"{L}_w1T", [D, FF]))
        setattr(g, f"in_{L}_w2T", inp(f"{L}_w2T", [D, FF]))
        setattr(g, f"in_{L}_h1T", inp(f"{L}_h1T", [FF, FF]))
        setattr(g, f"in_{L}_h2T", inp(f"{L}_h2T", [FF, FF]))
        setattr(g, f"in_{L}_w3T", inp(f"{L}_w3T", [FF, D]))
        for s in ("s1", "s2", "sh1", "sh2"):
            setattr(g, f"in_{L}_{s}", inp(f"{L}_{s}", [32, 128]))
        setattr(g, f"in_{L}_s3", inp(f"{L}_s3", [8, 128]))
    g.in_Wq = inp("Wq", [NB, D, D])
    g.in_Wk = inp("Wk", [NB, D, D])
    g.in_Wv = inp("Wv", [NB, D, D])
    g.in_Wo = inp("Wo", [NB, D, D])
    g.in_W1 = inp("W1", [NB, D, FF])
    g.in_W2 = inp("W2", [NB, FF, D])
    g.in_bq = inp("bq", [NB, 8, 128])
    g.in_bk = inp("bk", [NB, 8, 128])
    g.in_bv = inp("bv", [NB, 8, 128])
    g.in_bo = inp("bo", [NB, 8, 128])
    g.in_b1 = inp("b1", [NB, 32, 128])
    g.in_b2 = inp("b2", [NB, 8, 128])
    g.in_l1s = inp("ln1_s", [NB, 8, 128])
    g.in_l1b = inp("ln1_b", [NB, 8, 128])
    g.in_l2s = inp("ln2_s", [NB, 8, 128])
    g.in_l2b = inp("ln2_b", [NB, 8, 128])

    g.out_y = nc.dram_tensor("y_fm", [D, T], F32, kind="ExternalOutput").ap()

    g.handles = {}

    def scr(name, shape, dt=F32):
        h = nc.dram_tensor(name, list(shape), dt, kind="Internal")
        g.handles[name] = h
        return h.ap()

    g.ln1R = scr("ln1R", [D, T])
    g.X1 = scr("X1", [D, T])
    g.X2 = scr("X2", [D, T])
    g.X3 = scr("X3", [D, T])
    g.uPre = scr("uPre", [D, T])
    g.zPre = scr("zPre", [D, T])
    g.zR = scr("zR", [D, T])
    g.hbf = scr("hbf", [D, T], BF16)
    g.QB = scr("QB", [D, T], BF16)
    g.KB = scr("KB", [D, T], BF16)
    g.attnA = scr("attnA", [D, T], BF16)
    g.vaug = scr("vaug", [T, 1040], AB16)
    g.ffh = scr("ffh", [FF, 2 * T], U8).bitcast(BF16)
    g.ffg = scr("ffg", [FF, 2 * T], U8).bitcast(BF16)
    g.statraw = scr("statraw", [1, 2 * T])
    g.rowAB = scr("rowAB", [1, 2 * T])
    g.ident_hbm = nc.inline_tensor(np.eye(128, dtype=np.float32), name="ident").ap()

    g.persist = ExitStack()
    psb = lambda n, s, d: g.persist.enter_context(nc.sbuf_tensor(n, s, d))
    g.ones_col_bf = psb("ones_col_bf", [128, 1], BF16)
    g.onesf = psb("onesf", [1, 128], F32)
    g.bqkv_c = psb("bqkv_c", [128, 24], F32)
    g.bout_c = psb("bout_c", [128, 8], F32)
    g.gam_c = psb("gam_c", [128, 8], F32)
    g.bet_c = psb("bet_c", [128, 8], F32)
    for L in ("f0", "f1"):
        for s, w in (("s1", 32), ("s2", 32), ("sh1", 32), ("sh2", 32), ("s3", 8)):
            setattr(g, f"{L}_{s}c", psb(f"{L}_{s}c", [128, w], F32))
    for nmv, w in (("bq", 8), ("bk", 8), ("bv", 8), ("bo", 8), ("b2", 8),
                   ("l1s", 8), ("l1b", 8), ("l2s", 8), ("l2b", 8), ("b1", 32)):
        setattr(g, f"{nmv}_c", psb(f"{nmv}_c", [128, NB * w], F32))
    g.KVc = psb("KVc", [128, 8], F32)
    g.KSc = psb("KSc", [128, 8], F32)
    g.nbq_c = psb("nbq_c", [128, NB * 8], F32)
    g.nbk_c = psb("nbk_c", [128, NB * 8], F32)

    sc = []
    sc.append((g.in_b_qkv[:], g.bqkv_c[:, 0:24], 24))
    sc.append((g.in_b_out[:], g.bout_c[:, 0:8], 8))
    sc.append((g.in_gam[:], g.gam_c[:, 0:8], 8))
    sc.append((g.in_bet[:], g.bet_c[:, 0:8], 8))
    for L in ("f0", "f1"):
        for s, w in (("s1", 32), ("s2", 32), ("sh1", 32), ("sh2", 32), ("s3", 8)):
            sc.append((getattr(g, f"in_{L}_{s}")[:], getattr(g, f"{L}_{s}c")[:, 0:w], w))
    for nmv, w in (("bq", 8), ("bk", 8), ("bv", 8), ("bo", 8), ("b2", 8),
                   ("l1s", 8), ("l1b", 8), ("l2s", 8), ("l2b", 8), ("b1", 32)):
        src_t = getattr(g, f"in_{nmv}")
        dst = getattr(g, f"{nmv}_c")
        for i in range(NB):
            sc.append((src_t[i], dst[:, i * w:(i + 1) * w], w))
    g.setup_cols = sc

    stages = []
    stages.append(lambda: stage_setup(g))
    stages.append(lambda: stage_stats(g, g.in_x, "s0"))
    stages.append(lambda: stage_ln_finalize(g, 1e-5))
    stages.append(lambda: stage_apply(g, g.in_x, None, None, None, g.ln1R, tag="a0"))
    stages.append(lambda: stage_stats(g, g.ln1R, "s1"))
    stages.append(lambda: stage_ln_finalize(g, 1e-5))
    stages.append(lambda: stage_apply(g, g.ln1R, g.gam_c, g.bet_c, g.hbf, None, tag="a1"))
    extend_stages(g, stages)

    n = len(stages) if n_stage_limit is None else min(n_stage_limit, len(stages))
    for i in range(n):
        stages[i]()

    TAPS = {"ln1R": ([D, T], F32), "X1": ([D, T], F32), "X2": ([D, T], F32),
            "X3": ([D, T], F32), "uPre": ([D, T], F32), "zPre": ([D, T], F32),
            "zR": ([D, T], F32), "hbf": ([D, T], BF16), "QB": ([D, T], BF16),
            "KB": ([D, T], BF16), "attnA": ([D, T], BF16), "vaug": ([T, 1040], BF16),
            "ffh": ([FF, T], BF16), "ffg": ([FF, T], BF16),
            "rowAB": ([1, 2 * T], F32), "statraw": ([1, 2 * T], F32)}
    if debug_taps:
        with nc.reset_on_exit():
            with ExitStack() as st:
                sD = st.enter_context(nc.semaphore("sD"))
                nd = 0
                if "cols" in debug_taps:
                    debug_taps = [t for t in debug_taps if t != "cols"]
                    colout = nc.dram_tensor("tap_cols", [128, 48], F32, kind="ExternalOutput").ap()
                    cbuf = st.enter_context(nc.sbuf_tensor(g.u("cdump"), [128, 48], F32))
                    nc.vector.tensor_copy(cbuf[:, 0:24], g.bqkv_c[:])
                    nc.vector.tensor_copy(cbuf[:, 24:32], g.gam_c[:])
                    nc.vector.tensor_copy(cbuf[:, 32:40], g.bet_c[:])
                    nc.vector.tensor_copy(cbuf[:, 40:48], g.bout_c[:]).then_inc(sD, 1)
                    nc.sync.wait_ge(sD, 1)
                    nc.sync.dma_start(out=colout[:], in_=cbuf[:]).then_inc(sD, 16)
                    nd += 1
                    sD2_base = 1
                else:
                    sD2_base = 0
                for name in debug_taps:
                    srcb = getattr(g, name)
                    shp, dt_ = TAPS[name]
                    dst = nc.dram_tensor(f"tap_{name}", shp, F32, kind="ExternalOutput").ap()
                    if dt_ == BF16:
                        nc.gpsimd.dma_start(out=dst[:], in_=srcb[:]).then_inc(sD, 16)
                    else:
                        nc.sync.dma_start(out=dst[:], in_=srcb[:]).then_inc(sD, 16)
                    nd += 1
                nc.sync.wait_ge(sD, 16 * nd + sD2_base)
                nc.gpsimd.wait_ge(sD, 16 * nd + sD2_base)
    return nc, g


def stage_qkv_l0(g):
    nc = g.nc
    srcu = _fmu(g.hbf)
    qbu = _fmu(g.QB)
    kbu = _fmu(g.KB)
    va_flat = g.vaug.bitcast(U8).rearrange("a b -> (a b)")
    VROWB = 1040 * 2
    with nc.reset_on_exit():
        with ExitStack() as st:
            wq = st.enter_context(nc.sbuf_tensor(g.u("wqkv"), [128, NCH * 3 * D], BF16))
            xt = st.enter_context(nc.sbuf_tensor(g.u("qx"), [128, 2 * NCH * TT], BF16))
            qo = st.enter_context(nc.sbuf_tensor(g.u("qo"), [128, 2 * NCH * TT], BF16))
            ko = st.enter_context(nc.sbuf_tensor(g.u("ko"), [128, 2 * NCH * TT], BF16))
            vt = st.enter_context(nc.sbuf_tensor(g.u("vt"), [128, 2 * 1040], AB16))
            va = st.enter_context(nc.sbuf_tensor(g.u("va"), [128, 2 * 512], F32))
            bvr = st.enter_context(nc.sbuf_tensor(g.u("bvr"), [1, 1024], F32))
            pq = [st.enter_context(nc.psum_tensor(g.u(f"pq{i}"), [128, TT], F32)) for i in range(4)]
            pbv = [st.enter_context(nc.psum_tensor(g.u(f"pbv{i}"), [128, TT], F32)) for i in range(2)]
            sW = st.enter_context(nc.semaphore("sW"))
            sL = st.enter_context(nc.semaphore("sL"))
            sPE = st.enter_context(nc.semaphore("sPE"))
            sACT = st.enter_context(nc.semaphore("sACT"))
            sDVE = st.enter_context(nc.semaphore("sDVE"))
            sST = st.enter_context(nc.semaphore("sST"))
            rs = RegSet(g)
            for k in range(NCH):
                nc.gpsimd.dma_start(out=wq[:, k * 3 * D:(k + 1) * 3 * D],
                                    in_=g.in_w_qkv[k * 128:(k + 1) * 128, :]).then_inc(sW, 16)
            bq_flat = g.in_b_qkv.rearrange("r c -> (r c)")
            nc.sync.dma_start(out=bvr[:], in_=bq_flat[2048:3072][None, :]).then_inc(sL, 16)
            for p2 in range(2):
                nc.vector.memset(vt.rearrange("p (h c) -> p h c", c=65)[:, 16 * p2:16 * (p2 + 1), 64:65], 1.0)
            nc.vector.nop().then_inc(sDVE, 1)
            nc.tensor.wait_ge(sL, 16)
            nc.tensor.wait_ge(sW, 16 * NCH)
            nc.tensor.matmul(pbv[0][:], g.onesf[:], bvr[0:1, 0:512], start=True, stop=True)
            nc.tensor.matmul(pbv[1][:], g.onesf[:], bvr[0:1, 512:1024], start=True, stop=True).then_inc(sPE, 1)
            PB_PE, PB_A, PB_V, PB_ST = 48, 8, 2, 12
            nc.tensor.sem_inc(sPE, PB_PE)
            nc.scalar.sem_inc(sACT, PB_A)
            nc.vector.sem_inc(sDVE, PB_V - 1)
            nc.sync.sem_inc(sST, 16 * PB_ST)
            sy_wpe = rs.r(nc.sync, 25)
            sy_wdva = rs.r(nc.sync, 20)
            sy_wdqk = rs.r(nc.sync, 18)
            sy_oin = rs.r(nc.sync, 0, hi=(NTILE - 1) * TT * 2)
            sy_oq = rs.r(nc.sync, 0, hi=(NTILE - 1) * TT * 2)
            sy_ok = rs.r(nc.sync, 0, hi=(NTILE - 1) * TT * 2)
            sy_ova = rs.r(nc.sync, 0, hi=(T - 128) * 1040 * 2)
            te_wl = rs.r(nc.tensor, 32)
            te_wat0 = rs.r(nc.tensor, 0)
            te_wdve = rs.r(nc.tensor, 3)
            te_wact2 = rs.r(nc.tensor, PB_A + 1)
            sc_wpe = rs.r(nc.scalar, PB_PE + 1 + 17)
            sc_wva = rs.r(nc.scalar, 17)
            ve_wpe = rs.r(nc.vector, PB_PE + 1 + 1)
            ve_wact = rs.r(nc.vector, PB_A + 1)
            ve_wstqk = rs.r(nc.vector, 16 * 6)
            ve_wstv0 = rs.r(nc.vector, 16 * 4)
            ve_wstvs = rs.r(nc.vector, 16 * 13)
            with fori_clean(nc, NTILE // 2, [ET.SP, ET.Activation, ET.PE, ET.DVE]):
                for par in range(2):
                    o = par * NCH * TT
                    xtv = xt[:, o:o + NCH * TT]
                    qov = qo[:, o:o + NCH * TT]
                    kov = ko[:, o:o + NCH * TT]
                    vtv = vt[:, par * 1040:(par + 1) * 1040]
                    vav = va
                    nc.sync.wait_ge(sPE, sy_wpe.h)
                    nc.sync.dma_start(out=xtv.bitcast(U8),
                                      in_=srcu[:, :, bass.DynSlice(sy_oin.v, TT * 2)]).then_inc(sL, 16)
                    nc.tensor.wait_ge(sL, te_wl.h)
                    gidx = 0
                    for proj, base in ((0, 0), (1, D)):
                        for m in range(NCH):
                            pb = pq[2 * par + (gidx % 2)]
                            if gidx >= 2:
                                nc.tensor.wait_ge(sDVE, te_wdve.h)
                                te_wdve.bump(1)
                            else:
                                nc.tensor.wait_ge(sACT, te_wat0.h)
                            for k in range(NCH):
                                ins = nc.tensor.matmul(
                                    pb[:], wq[:, k * 3 * D + base + m * 128:k * 3 * D + base + m * 128 + 128],
                                    xtv[:, k * TT:(k + 1) * TT],
                                    start=(k == 0), stop=(k == NCH - 1))
                            ins.then_inc(sPE, 1)
                            dstv = qov if proj == 0 else kov
                            bc = g.bqkv_c[:, (base // 128) + m:(base // 128) + m + 1]
                            nc.vector.wait_ge(sPE, ve_wpe.h)
                            ve_wpe.bump(1)
                            if gidx == 0:
                                nc.vector.wait_ge(sST, ve_wstqk.h)
                            nc.vector.tensor_scalar(dstv[:, m * TT:(m + 1) * TT], pb[:],
                                                    bc, None, ALU.add).then_inc(sDVE, 1)
                            gidx += 1
                    for s in range(4):
                        for half in range(2):
                            gg = 16 + 2 * s + half
                            pb = pq[2 * par + (gg % 2)]
                            if gg < 18:
                                nc.tensor.wait_ge(sDVE, te_wdve.h)
                                te_wdve.bump(1)
                            else:
                                nc.tensor.wait_ge(sACT, te_wact2.h)
                                te_wact2.bump(1)
                            for k in range(NCH):
                                ins = nc.tensor.matmul(
                                    pb[:], xtv[:, k * TT + s * 128:k * TT + s * 128 + 128],
                                    wq[:, k * 3 * D + 2 * D + half * 512:k * 3 * D + 2 * D + (half + 1) * 512],
                                    start=(k == 0), stop=(k == NCH - 1))
                            ins.then_inc(sPE, 1)
                            nc.scalar.wait_ge(sPE, sc_wpe.h)
                            sc_wpe.bump(1)
                            nc.scalar.wait_ge(sDVE, sc_wva.h)
                            sc_wva.bump(1)
                            nc.scalar.copy(vav[:, half * 512:(half + 1) * 512], pb[:]).then_inc(sACT, 1)
                            nc.vector.wait_ge(sACT, ve_wact.h)
                            ve_wact.bump(1)
                            if half == 0:
                                if s == 0:
                                    nc.vector.wait_ge(sST, ve_wstv0.h)
                                else:
                                    nc.vector.wait_ge(sST, ve_wstvs.h)
                                    ve_wstvs.bump(16)
                            vv = vtv.rearrange("p (h c) -> p h c", c=65)[:, 8 * half:8 * (half + 1), 0:64]
                            nc.vector.tensor_add(
                                vv,
                                vav.rearrange("p (h c) -> p h c", c=64)[:, 8 * half:8 * (half + 1), :],
                                pbv[half][:, 0:512].rearrange("p (h c) -> p h c", c=64)).then_inc(sDVE, 1)
                        nc.sync.wait_ge(sDVE, sy_wdva.h)
                        sy_wdva.bump(2)
                        nc.sync.dma_start(
                            out=va_flat[bass.DynSlice(sy_ova.v, 128 * VROWB)].rearrange("(p n) -> p n", p=128),
                            in_=vtv.bitcast(U8)).then_inc(sST, 16)
                        sy_ova.bump(128 * VROWB)
                    nc.sync.wait_ge(sDVE, sy_wdqk.h)
                    nc.sync.dma_start(out=qbu[:, :, bass.DynSlice(sy_oq.v, TT * 2)],
                                      in_=qov.bitcast(U8)).then_inc(sST, 16)
                    nc.sync.dma_start(out=kbu[:, :, bass.DynSlice(sy_ok.v, TT * 2)],
                                      in_=kov.bitcast(U8)).then_inc(sST, 16)
                    sy_wpe.bump(24); sy_wdva.bump(16); sy_wdqk.bump(24)
                    sy_oin.bump(TT * 2); sy_oq.bump(TT * 2); sy_ok.bump(TT * 2)
                    te_wl.bump(16); te_wat0.bump(8); te_wdve.bump(8); te_wact2.bump(2)
                    sc_wpe.bump(16); sc_wva.bump(16)
                    ve_wpe.bump(8); ve_wact.bump(0); ve_wstqk.bump(16 * 6)
                    ve_wstv0.bump(16 * 6); ve_wstvs.bump(16 * 3)
            nc.sync.wait_ge(sST, 16 * (PB_ST + 6 * NTILE))
            rs.close()


def stage_attn_l0(g):
    nc = g.nc
    NKC = 32
    NSP = 4
    with nc.reset_on_exit():
        with ExitStack() as st:
            vau = st.enter_context(nc.sbuf_tensor(g.u("vau"), [128, 32 * 1040], AB16))
            qh = st.enter_context(nc.sbuf_tensor(g.u("qh"), [64, 2 * T], BF16))
            kh = st.enter_context(nc.sbuf_tensor(g.u("kh"), [64, 2 * T], BF16))
            sq = st.enter_context(nc.sbuf_tensor(g.u("sq"), [64, T], BF16))
            nrow = st.enter_context(nc.sbuf_tensor(g.u("nrow"), [1, T], F32))
            mrow = st.enter_context(nc.sbuf_tensor(g.u("mrow"), [1, 8], F32))
            nbc = st.enter_context(nc.sbuf_tensor(g.u("nbc"), [128, 1], F32))
            pT = [st.enter_context(nc.sbuf_tensor(g.u(f"pT{u}"), [128, 1024], AB16)) for u in range(2)]
            invs = st.enter_context(nc.sbuf_tensor(g.u("invs"), [1, 1024], F32))
            aof = st.enter_context(nc.sbuf_tensor(g.u("aof"), [64, 1024], F32))
            aob = st.enter_context(nc.sbuf_tensor(g.u("aob"), [64, 2 * 1024], BF16))
            pS = [st.enter_context(nc.psum_tensor(g.u(f"pS{u}"), [128, 1024], F32)) for u in range(2)]
            pPV = st.enter_context(nc.psum_tensor(g.u("pPV"), [65, 1024], F32))
            pB = st.enter_context(nc.psum_tensor(g.u("pB"), [128, 1024], F32))
            sLD = st.enter_context(nc.semaphore("sLD"))
            sPE = st.enter_context(nc.semaphore("sPE"))
            sA = st.enter_context(nc.semaphore("sA"))
            sV = st.enter_context(nc.semaphore("sV"))
            sS = st.enter_context(nc.semaphore("sS"))
            c = {"L": 0, "P": 0, "A": 0, "V": 0, "S": 0}

            def iL(ins):
                c["L"] += 16
                ins.then_inc(sLD, 16)
            def iP(ins):
                c["P"] += 1
                ins.then_inc(sPE, 1)
            def iA(ins):
                c["A"] += 1
                ins.then_inc(sA, 1)
            def iV(ins):
                c["V"] += 1
                ins.then_inc(sV, 1)
            def iS(ins):
                c["S"] += 16
                ins.then_inc(sS, 16)

            iL(nc.sync.dma_start(out=vau[:], in_=_fm(g.vaug)))
            ev = {}
            for h in range(NHEAD):
                hb = h % 2
                if h >= 2:
                    nc.sync.wait_ge(sPE, ev[("pe_end", h - 1)])
                iL(nc.sync.dma_start(out=qh[:, hb * T:(hb + 1) * T],
                                     in_=g.QB[h * 64:(h + 1) * 64, :]))
                iL(nc.sync.dma_start(out=kh[:, hb * T:(hb + 1) * T],
                                     in_=g.KB[h * 64:(h + 1) * 64, :]))
                ld_done = c["L"]
                qv = qh[:, hb * T:(hb + 1) * T]
                kv = kh[:, hb * T:(hb + 1) * T]

                nc.scalar.wait_ge(sLD, ld_done)
                if h >= 1:
                    nc.scalar.wait_ge(sPE, ev[("knorm", h - 1)])
                iA(nc.scalar.activation(sq[:], qv, AF.Square))
                slots = [pS[0][0:1, 0:512], pS[0][0:1, 512:1024], pS[1][0:1, 0:512], pS[1][0:1, 512:1024]]
                if h >= 1:
                    nc.tensor.wait_ge(sA, ev[("exp_end", h - 1)])
                for phase in range(2):
                    nc.tensor.wait_ge(sA, c["A"])
                    for j in range(NCH):
                        if phase * NCH + j >= 4:
                            nc.tensor.wait_ge(sV, ev[("nevac", h, phase * NCH + j - 4)])
                        iP(nc.tensor.matmul(slots[j % 4], g.ones_col_bf[0:64, :],
                                            sq[:, j * 512:(j + 1) * 512], start=True, stop=True))
                        nc.vector.wait_ge(sPE, c["P"])
                        iV(nc.vector.tensor_copy(nrow[0:1, j * 512:(j + 1) * 512], slots[j % 4]))
                        ev[("nevac", h, phase * NCH + j)] = c["V"]
                    if phase == 0:
                        ev[("qnorm", h)] = c["P"]
                        iV(nc.vector.reduce_max(mrow[0:1, 0:1], nrow[:], axis=AX.X))
                        nc.scalar.wait_ge(sV, c["V"])
                        iA(nc.scalar.activation(mrow[0:1, 1:2], mrow[0:1, 0:1], AF.Ln))
                        iA(nc.scalar.activation(sq[:], kv, AF.Square))
                    else:
                        ev[("knorm", h)] = c["P"]
                        iV(nc.vector.reduce_max(mrow[0:1, 2:3], nrow[:], axis=AX.X))
                        nc.scalar.wait_ge(sV, c["V"])
                        iA(nc.scalar.activation(mrow[0:1, 3:4], mrow[0:1, 2:3], AF.Ln))
                nc.vector.wait_ge(sA, c["A"])
                iV(nc.vector.tensor_add(mrow[0:1, 4:5], mrow[0:1, 1:2], mrow[0:1, 3:4]))
                nc.scalar.wait_ge(sV, c["V"])
                iA(nc.scalar.activation(mrow[0:1, 5:6], mrow[0:1, 4:5], AF.Exp, scale=0.5))
                nc.vector.wait_ge(sA, c["A"])
                iV(nc.vector.tensor_scalar_mul(mrow[0:1, 6:7], mrow[0:1, 5:6], -0.125))
                nc.tensor.wait_ge(sV, c["V"])
                if h >= 1:
                    nc.tensor.wait_ge(sV, ev[("fin_end", h - 1)])
                iP(nc.tensor.matmul(pB[:, 0:1], g.onesf[:], mrow[0:1, 6:7], start=True, stop=True))
                nc.scalar.wait_ge(sPE, c["P"])
                if h >= 1:
                    nc.scalar.wait_ge(sA, ev[("exp_end", h - 1)])
                iA(nc.scalar.copy(nbc[:], pB[:, 0:1]))
                ev[("negB", h)] = c["A"]

                for sp in range(NSP):
                    exp_cnt_base = c["A"]
                    pv_done = {}
                    for kc in range(NKC + 1):
                        if kc < NKC:
                            u = kc % 2
                            if kc >= 2:
                                nc.tensor.wait_ge(sA, ev[("exp", h, sp, kc - 2)])
                            elif sp > 0:
                                nc.tensor.wait_ge(sA, ev[("exp", h, sp - 1, NKC - 2 + kc)])
                            for qq in range(2):
                                iP(nc.tensor.matmul(
                                    pS[u][:, qq * 512:(qq + 1) * 512],
                                    kv[:, kc * 128:(kc + 1) * 128],
                                    qv[:, sp * 1024 + qq * 512: sp * 1024 + (qq + 1) * 512],
                                    start=True, stop=True))
                            ev[("sc", h, sp, kc)] = c["P"]
                            nc.scalar.wait_ge(sPE, ev[("sc", h, sp, kc)])
                            if kc >= 2:
                                nc.scalar.wait_ge(sPE, ev[("pv", h, sp, kc - 2)])
                            iA(nc.scalar.activation(pT[u][:], pS[u][:], AF.Exp,
                                                    bias=nbc[:], scale=0.125))
                            ev[("exp", h, sp, kc)] = c["A"]
                        if kc >= 1:
                            kcp = kc - 1
                            u = kcp % 2
                            nc.tensor.wait_ge(sA, ev[("exp", h, sp, kcp)])
                            if kcp == 0 and ("aof_last" in ev):
                                nc.tensor.wait_ge(sA, ev["aof_last"])
                            for qq in range(2):
                                iP(nc.tensor.matmul(
                                    pPV[:, qq * 512:(qq + 1) * 512],
                                    vau[:, kcp * 1040 + h * 65: kcp * 1040 + (h + 1) * 65],
                                    pT[u][:, qq * 512:(qq + 1) * 512],
                                    start=(kcp == 0), stop=(kcp == NKC - 1)))
                            ev[("pv", h, sp, kcp)] = c["P"]
                    ev[("exp_end_sp", h, sp)] = c["A"]
                    nc.vector.wait_ge(sPE, ev[("pv", h, sp, NKC - 1)])
                    iV(nc.vector.reciprocal(invs[0:1, 0:512], pPV[64:65, 0:512]))
                    iV(nc.vector.reciprocal(invs[0:1, 512:1024], pPV[64:65, 512:1024]))
                    nc.tensor.wait_ge(sV, c["V"])
                    for qq in range(2):
                        iP(nc.tensor.matmul(pB[0:64, qq * 512:(qq + 1) * 512], g.onesf[0:1, 0:64],
                                            invs[0:1, qq * 512:(qq + 1) * 512], start=True, stop=True))
                    nc.scalar.wait_ge(sPE, c["P"] - 2)
                    iA(nc.scalar.copy(aof[:], pPV[0:64, :]))
                    ev["aof_last"] = c["A"]
                    nc.vector.wait_ge(sA, c["A"])
                    nc.vector.wait_ge(sPE, c["P"])
                    if (h * NSP + sp) >= 2:
                        nc.vector.wait_ge(sS, 16 * (h * NSP + sp - 1))
                    sl = ((h * NSP + sp) % 2) * 1024
                    iV(nc.vector.tensor_mul(aob[:, sl:sl + 1024], aof[:], pB[0:64, :]))
                    ev[("fin_end", h)] = c["V"]
                    nc.sync.wait_ge(sV, c["V"])
                    iS(nc.sync.dma_start(out=g.attnA[h * 64:(h + 1) * 64, sp * 1024:(sp + 1) * 1024],
                                         in_=aob[:, sl:sl + 1024]))
                ev[("exp_end", h)] = c["A"]
                ev[("pe_end", h)] = c["P"]
            nc.sync.wait_ge(sS, c["S"])


def stage_wout_l0(g):
    nc = g.nc
    srcu = _fmu(g.attnA)
    xu = _fmu(g.in_x)
    lu = _fmu(g.ln1R)
    outu = _fmu(g.X1)
    TTB = TT * 4
    with nc.reset_on_exit():
        with ExitStack() as st:
            wo = st.enter_context(nc.sbuf_tensor(g.u("wo"), [128, NCH * D], BF16))
            at = st.enter_context(nc.sbuf_tensor(g.u("at"), [128, 2 * NCH * TT], BF16))
            xt = st.enter_context(nc.sbuf_tensor(g.u("xt"), [128, 2 * NCH * TT], F32))
            lt = st.enter_context(nc.sbuf_tensor(g.u("lt"), [128, 2 * NCH * TT], F32))
            yf = st.enter_context(nc.sbuf_tensor(g.u("yf"), [128, 2 * NCH * TT], F32))
            pq = [st.enter_context(nc.psum_tensor(g.u(f"pw{i}"), [128, TT], F32)) for i in range(4)]
            sW = st.enter_context(nc.semaphore("sW"))
            sL = st.enter_context(nc.semaphore("sL"))
            sPE = st.enter_context(nc.semaphore("sPE"))
            sV = st.enter_context(nc.semaphore("sV"))
            sS = st.enter_context(nc.semaphore("sS"))
            rs = RegSet(g)
            for k in range(NCH):
                nc.gpsimd.dma_start(out=wo[:, k * D:(k + 1) * D],
                                    in_=g.in_w_out[k * 128:(k + 1) * 128, :]).then_inc(sW, 16)
            nc.tensor.wait_ge(sW, 16 * NCH)
            PB_PE, PB_V, PB_S = 16, 48, 2
            nc.tensor.sem_inc(sPE, PB_PE)
            nc.vector.sem_inc(sV, PB_V)
            nc.sync.sem_inc(sS, 16 * PB_S)
            sy_wv = rs.r(nc.sync, 24)
            sy_wv2 = rs.r(nc.sync, PB_V + 24)
            sy_oa = rs.r(nc.sync, 0, hi=(NTILE - 1) * TT * 2)
            sy_ox = rs.r(nc.sync, 0, hi=(NTILE - 1) * TTB)
            sy_ol = rs.r(nc.sync, 0, hi=(NTILE - 1) * TTB)
            sy_oo = rs.r(nc.sync, 0, hi=(NTILE - 1) * TTB)
            te_wl = rs.r(nc.tensor, 16 * 3)
            te_wv = rs.r(nc.tensor, 51)
            ve_wp = rs.r(nc.vector, PB_PE + 1)
            ve_ws = rs.r(nc.vector, 16 * 1)
            with fori_clean(nc, NTILE // 2, [ET.SP, ET.PE, ET.DVE]):
                for par in range(2):
                    o = par * NCH * TT
                    atv, xtv, ltv, yfv = at[:, o:o + NCH * TT], xt[:, o:o + NCH * TT], lt[:, o:o + NCH * TT], yf[:, o:o + NCH * TT]
                    nc.sync.wait_ge(sV, sy_wv.h)
                    nc.sync.dma_start(out=atv.bitcast(U8),
                                      in_=srcu[:, :, bass.DynSlice(sy_oa.v, TT * 2)]).then_inc(sL, 16)
                    nc.sync.dma_start(out=xtv.bitcast(U8),
                                      in_=xu[:, :, bass.DynSlice(sy_ox.v, TTB)]).then_inc(sL, 16)
                    nc.sync.dma_start(out=ltv.bitcast(U8),
                                      in_=lu[:, :, bass.DynSlice(sy_ol.v, TTB)]).then_inc(sL, 16)
                    nc.tensor.wait_ge(sL, te_wl.h)
                    for m in range(NCH):
                        pb = pq[2 * par + (m % 2)]
                        if m >= 2:
                            nc.tensor.wait_ge(sV, te_wv.h)
                            te_wv.bump(3)
                        for k in range(NCH):
                            ins = nc.tensor.matmul(pb[:], wo[:, k * D + m * 128:k * D + m * 128 + 128],
                                                   atv[:, k * TT:(k + 1) * TT],
                                                   start=(k == 0), stop=(k == NCH - 1))
                        ins.then_inc(sPE, 1)
                        nc.vector.wait_ge(sPE, ve_wp.h)
                        ve_wp.bump(1)
                        if m == 0:
                            nc.vector.wait_ge(sS, ve_ws.h)
                        s0, s1 = m * TT, (m + 1) * TT
                        nc.vector.tensor_scalar(yfv[:, s0:s1], pb[:], g.bout_c[:, m:m + 1], None, ALU.add)
                        nc.vector.tensor_add(yfv[:, s0:s1], yfv[:, s0:s1], xtv[:, s0:s1])
                        nc.vector.tensor_add(yfv[:, s0:s1], yfv[:, s0:s1], ltv[:, s0:s1]).then_inc(sV, 3)
                    nc.sync.wait_ge(sV, sy_wv2.h)
                    nc.sync.dma_start(out=outu[:, :, bass.DynSlice(sy_oo.v, TTB)],
                                      in_=yfv.bitcast(U8)).then_inc(sS, 16)
                    sy_wv.bump(24); sy_wv2.bump(24)
                    sy_oa.bump(TT * 2); sy_ox.bump(TTB); sy_ol.bump(TTB); sy_oo.bump(TTB)
                    te_wl.bump(48); te_wv.bump(6)
                    ve_ws.bump(16)
            nc.sync.wait_ge(sS, 16 * (PB_S + NTILE))
            rs.close()


def stage_ffn_a(g, w1T, w2T, s1c, s2c, src_bf, dst_bf, dst_h, tag):
    nc = g.nc
    srcu = _fmu(src_bf)
    dstu = _fmu(dst_bf)
    NG = 64
    with nc.reset_on_exit():
        with ExitStack() as st:
            w1 = st.enter_context(nc.sbuf_tensor(g.u(f"w1{tag}"), [128, NCH * FF], BF16))
            w2 = st.enter_context(nc.sbuf_tensor(g.u(f"w2{tag}"), [128, NCH * FF], BF16))
            xt = st.enter_context(nc.sbuf_tensor(g.u(f"xt{tag}"), [128, 2 * NCH * TT], BF16))
            a1 = st.enter_context(nc.sbuf_tensor(g.u(f"a1{tag}"), [128, 2 * TT], F32))
            a2 = st.enter_context(nc.sbuf_tensor(g.u(f"a2{tag}"), [128, 2 * TT], F32))
            ht = st.enter_context(nc.sbuf_tensor(g.u(f"ht{tag}"), [128, 2 * 16 * TT], BF16))
            pq = [st.enter_context(nc.psum_tensor(g.u(f"pf{i}{tag}"), [128, TT], F32)) for i in range(4)]
            sW = st.enter_context(nc.semaphore("sW"))
            sL = st.enter_context(nc.semaphore("sL"))
            sPE = st.enter_context(nc.semaphore("sPE"))
            sA = st.enter_context(nc.semaphore("sA"))
            sV = st.enter_context(nc.semaphore("sV"))
            sS = st.enter_context(nc.semaphore("sS"))
            rs = RegSet(g)
            for k in range(NCH):
                nc.gpsimd.dma_start(out=w1[:, k * FF:(k + 1) * FF],
                                    in_=w1T[k * 128:(k + 1) * 128, :]).then_inc(sW, 16)
                nc.gpsimd.dma_start(out=w2[:, k * FF:(k + 1) * FF],
                                    in_=w2T[k * 128:(k + 1) * 128, :]).then_inc(sW, 16)
            nc.tensor.wait_ge(sW, 16 * 2 * NCH)
            PB_PE, PB_A, PB_V, PB_S = 2 * NG, 2, 2, 4
            nc.tensor.sem_inc(sPE, PB_PE)
            nc.scalar.sem_inc(sA, PB_A)
            nc.vector.sem_inc(sV, PB_V)
            nc.sync.sem_inc(sS, 16 * PB_S)
            sy_wpe = rs.r(nc.sync, NG)
            sy_wv = rs.r(nc.sync, PB_V + 16)
            sy_oin = rs.r(nc.sync, 0, hi=(NTILE - 1) * TT * 2)
            CHB = 128 * T * 2
            sy_oh = [rs.r(nc.sync, hf * 16 * CHB, hi=hf * 16 * CHB + (NTILE - 1) * TT * 2)
                     for hf in range(2)]
            te_wl = rs.r(nc.tensor, 16)
            te_wa0 = rs.r(nc.tensor, PB_A)
            te_wa = rs.r(nc.tensor, PB_A + 1)
            sc_wpe = rs.r(nc.scalar, PB_PE + 1)
            sc_wv = rs.r(nc.scalar, 1)
            ve_wa = rs.r(nc.vector, PB_A + 2)
            ve_ws = rs.r(nc.vector, 16 * 3)
            with fori_clean(nc, NTILE // 2, [ET.SP, ET.Activation, ET.PE, ET.DVE]):
                for par in range(2):
                    o = par * NCH * TT
                    xtv = xt[:, o:o + NCH * TT]
                    nc.sync.wait_ge(sPE, sy_wpe.h)
                    nc.sync.dma_start(out=xtv.bitcast(U8),
                                      in_=srcu[:, :, bass.DynSlice(sy_oin.v, TT * 2)]).then_inc(sL, 16)
                    nc.tensor.wait_ge(sL, te_wl.h)
                    nc.tensor.wait_ge(sA, te_wa0.h)
                    for half in range(2):
                        hb = (2 * (0) + half) % 2
                        htv = ht[:, ((par * 2 + half) % 2) * 16 * TT:(((par * 2 + half) % 2) + 1) * 16 * TT]
                        for mm in range(16):
                            m = half * 16 + mm
                            for w_i, (wsb, scol, fn) in enumerate(((w1, s1c, AF.Silu), (w2, s2c, AF.Copy))):
                                gidx = 2 * m + w_i
                                pb = pq[2 * par + (gidx % 2)]
                                if gidx >= 2:
                                    nc.tensor.wait_ge(sA, te_wa.h)
                                    te_wa.bump(1)
                                for k in range(NCH):
                                    ins = nc.tensor.matmul(pb[:], wsb[:, k * FF + m * 128:k * FF + m * 128 + 128],
                                                           xtv[:, k * TT:(k + 1) * TT],
                                                           start=(k == 0), stop=(k == NCH - 1))
                                ins.then_inc(sPE, 1)
                                av = a1 if w_i == 0 else a2
                                nc.scalar.wait_ge(sPE, sc_wpe.h)
                                sc_wpe.bump(1)
                                nc.scalar.wait_ge(sV, sc_wv.h)
                                if w_i == 1:
                                    sc_wv.bump(1)
                                ins = nc.scalar.activation(av[:, (m % 2) * TT:(m % 2 + 1) * TT], pb[:],
                                                           fn, scale=scol[:, m:m + 1])
                                ins.then_inc(sA, 1)
                            nc.vector.wait_ge(sA, ve_wa.h)
                            ve_wa.bump(2)
                            if mm == 0:
                                nc.vector.wait_ge(sS, ve_ws.h)
                            nc.vector.tensor_mul(htv[:, mm * TT:(mm + 1) * TT],
                                                 a1[:, (m % 2) * TT:(m % 2 + 1) * TT],
                                                 a2[:, (m % 2) * TT:(m % 2 + 1) * TT]).then_inc(sV, 1)
                        nc.sync.wait_ge(sV, sy_wv.h)
                        sy_wv.bump(16)
                        nc.sync.dma_start(
                            out=bass.AP(dst_h, sy_oh[half].v,
                                        [[T * 2, 128], [CHB, 16], [1, TT * 2]]),
                            in_=htv.bitcast(U8)).then_inc(sS, 16)
                        nc.vector.engine_nop()
                        ve_ws.bump(16)
                    sy_wpe.bump(NG)
                    sy_oin.bump(TT * 2); sy_oh[0].bump(TT * 2); sy_oh[1].bump(TT * 2)
                    te_wl.bump(16); te_wa0.bump(NG); te_wa.bump(2)
            nc.sync.wait_ge(sS, 16 * (PB_S + 2 * NTILE))
            rs.close()


def stage_ffn_mid(g, hT, shc, src_bf, dst_bf, dst_h, tag):
    nc = g.nc
    srcu = _fmu(src_bf)
    dstu = _fmu(dst_bf)
    with nc.reset_on_exit():
        with ExitStack() as st:
            wsb = st.enter_context(nc.sbuf_tensor(g.u(f"wm{tag}"), [128, NFF * 2048], BF16))
            xt = st.enter_context(nc.sbuf_tensor(g.u(f"xm{tag}"), [128, 2 * NFF * TT], BF16))
            ot = st.enter_context(nc.sbuf_tensor(g.u(f"om{tag}"), [128, 2 * TT], BF16))
            pq = [st.enter_context(nc.psum_tensor(g.u(f"pm{i}{tag}"), [128, TT], F32)) for i in range(4)]
            sW = st.enter_context(nc.semaphore("sW"))
            sL = st.enter_context(nc.semaphore("sL"))
            sPE = st.enter_context(nc.semaphore("sPE"))
            sA = st.enter_context(nc.semaphore("sA"))
            sS = st.enter_context(nc.semaphore("sS"))
            rs = RegSet(g)
            PB_PE, PB_A, PB_S = 32, 34, 2
            nc.tensor.sem_inc(sPE, PB_PE)
            nc.scalar.sem_inc(sA, PB_A)
            nc.sync.sem_inc(sS, 16 * PB_S)
            cw = [0]
            cs_fin = [0]
            for half in range(2):
                if half == 1:
                    nc.gpsimd.wait_ge(sPE, PB_PE + 16 * NTILE)
                for k in range(NFF):
                    nc.gpsimd.dma_start(out=wsb[:, k * 2048:(k + 1) * 2048],
                                        in_=hT[k * 128:(k + 1) * 128, half * 2048:(half + 1) * 2048]).then_inc(sW, 16)
                cw[0] += 16 * NFF
                nc.tensor.wait_ge(sW, cw[0])
                rs2 = RegSet(g)
                sy_wpe = rs2.r(nc.sync, 16 * (half * NTILE + 1))
                sy_oin = rs2.r(nc.sync, 0, hi=(NTILE - 1) * TT * 2)
                sy_wa = rs2.r(nc.sync, PB_A + 1 + 16 * half * NTILE)
                te_wl = rs2.r(nc.tensor, 16 * (half * NTILE + 1))
                te_wa0 = rs2.r(nc.tensor, PB_A + 16 * half * NTILE)
                te_wa = rs2.r(nc.tensor, PB_A + 1 + 16 * half * NTILE)
                sc_wpe = rs2.r(nc.scalar, PB_PE + 1 + 16 * half * NTILE)
                sc_ws = rs2.r(nc.scalar, 16 * (1 + half * NTILE * 16))
                CHB = 128 * T * 2
                sy_oo2 = rs2.r(nc.sync, half * 16 * CHB, hi=31 * CHB + (NTILE - 1) * TT * 2)
                with fori_clean(nc, NTILE // 2, [ET.SP, ET.Activation, ET.PE]):
                    for par in range(2):
                        o = par * NFF * TT
                        xtv = xt[:, o:o + NFF * TT]
                        nc.sync.wait_ge(sPE, sy_wpe.h)
                        nc.sync.dma_start(out=xtv.bitcast(U8),
                                          in_=srcu[:, :, bass.DynSlice(sy_oin.v, TT * 2)]).then_inc(sL, 16)
                        nc.tensor.wait_ge(sL, te_wl.h)
                        nc.tensor.wait_ge(sA, te_wa0.h)
                        for m in range(16):
                            pb = pq[2 * par + (m % 2)]
                            if m >= 2:
                                nc.tensor.wait_ge(sA, te_wa.h)
                                te_wa.bump(1)
                            for k in range(NFF):
                                ins = nc.tensor.matmul(pb[:], wsb[:, k * 2048 + m * 128:k * 2048 + m * 128 + 128],
                                                       xtv[:, k * TT:(k + 1) * TT],
                                                       start=(k == 0), stop=(k == NFF - 1))
                            ins.then_inc(sPE, 1)
                            nc.scalar.wait_ge(sPE, sc_wpe.h)
                            sc_wpe.bump(1)
                            nc.scalar.wait_ge(sS, sc_ws.h)
                            sc_ws.bump(16)
                            nc.scalar.activation(ot[:, (m % 2) * TT:(m % 2 + 1) * TT], pb[:],
                                                 AF.Gelu, scale=shc[:, half * 16 + m:half * 16 + m + 1]).then_inc(sA, 1)
                            nc.sync.wait_ge(sA, sy_wa.h)
                            sy_wa.bump(1)
                            nc.sync.dma_start(
                                out=bass.AP(dst_h, sy_oo2.v, [[T * 2, 128], [1, TT * 2]]),
                                in_=ot[:, (m % 2) * TT:(m % 2 + 1) * TT].bitcast(U8)).then_inc(sS, 16)
                            sy_oo2.bump(CHB)
                        sy_wpe.bump(16)
                        sy_oin.bump(TT * 2)
                        nc.sync.reg_sub(sy_oo2.h, sy_oo2.h, 16 * CHB - TT * 2)
                        te_wl.bump(16); te_wa0.bump(16); te_wa.bump(2)
                rs2.close()
            nc.sync.wait_ge(sS, 16 * (PB_S + 2 * 16 * NTILE))
            rs.close()


def stage_ffn_c(g, w3T, s3c, src_bf, res_f32, dst_f32, tag):
    nc = g.nc
    srcu = _fmu(src_bf)
    resu = _fmu(res_f32)
    outu = _fmu(dst_f32)
    TTB = TT * 4
    with nc.reset_on_exit():
        with ExitStack() as st:
            wsb = st.enter_context(nc.sbuf_tensor(g.u(f"w3{tag}"), [128, NFF * D], BF16))
            xt = st.enter_context(nc.sbuf_tensor(g.u(f"xc{tag}"), [128, 2 * NFF * TT], BF16))
            rt = st.enter_context(nc.sbuf_tensor(g.u(f"rc{tag}"), [128, 2 * NCH * TT], F32))
            yf = st.enter_context(nc.sbuf_tensor(g.u(f"yc{tag}"), [128, 2 * NCH * TT], F32))
            pq = [st.enter_context(nc.psum_tensor(g.u(f"pc{i}{tag}"), [128, TT], F32)) for i in range(4)]
            sW = st.enter_context(nc.semaphore("sW"))
            sL = st.enter_context(nc.semaphore("sL"))
            sPE = st.enter_context(nc.semaphore("sPE"))
            sV = st.enter_context(nc.semaphore("sV"))
            sS = st.enter_context(nc.semaphore("sS"))
            rs = RegSet(g)
            for k in range(NFF):
                nc.gpsimd.dma_start(out=wsb[:, k * D:(k + 1) * D],
                                    in_=w3T[k * 128:(k + 1) * 128, :]).then_inc(sW, 16)
            nc.tensor.wait_ge(sW, 16 * NFF)
            PB_PE, PB_V, PB_S = 16, 32, 2
            nc.tensor.sem_inc(sPE, PB_PE)
            nc.vector.sem_inc(sV, PB_V)
            nc.sync.sem_inc(sS, 16 * PB_S)
            sy_wv = rs.r(nc.sync, 16)
            sy_wv2 = rs.r(nc.sync, PB_V + 16)
            sy_oin = rs.r(nc.sync, 0, hi=(NTILE - 1) * TT * 2)
            sy_or = rs.r(nc.sync, 0, hi=(NTILE - 1) * TTB)
            sy_oo = rs.r(nc.sync, 0, hi=(NTILE - 1) * TTB)
            te_wl = rs.r(nc.tensor, 32)
            te_wv0 = rs.r(nc.tensor, PB_V)
            te_wv = rs.r(nc.tensor, PB_V + 2)
            ve_wp = rs.r(nc.vector, PB_PE + 1)
            ve_ws = rs.r(nc.vector, 16 * 1)
            with fori_clean(nc, NTILE // 2, [ET.SP, ET.PE, ET.DVE]):
                for par in range(2):
                    o8 = par * NCH * TT
                    o32 = par * NFF * TT
                    xtv = xt[:, o32:o32 + NFF * TT]
                    rtv = rt[:, o8:o8 + NCH * TT]
                    yfv = yf[:, o8:o8 + NCH * TT]
                    nc.sync.wait_ge(sV, sy_wv.h)
                    nc.sync.dma_start(out=xtv.bitcast(U8),
                                      in_=srcu[:, :, bass.DynSlice(sy_oin.v, TT * 2)]).then_inc(sL, 16)
                    nc.sync.dma_start(out=rtv.bitcast(U8),
                                      in_=resu[:, :, bass.DynSlice(sy_or.v, TTB)]).then_inc(sL, 16)
                    nc.tensor.wait_ge(sL, te_wl.h)
                    nc.tensor.wait_ge(sV, te_wv0.h)
                    for m in range(NCH):
                        pb = pq[2 * par + (m % 2)]
                        if m >= 2:
                            nc.tensor.wait_ge(sV, te_wv.h)
                            te_wv.bump(2)
                        for k in range(NFF):
                            ins = nc.tensor.matmul(pb[:], wsb[:, k * D + m * 128:k * D + m * 128 + 128],
                                                   xtv[:, k * TT:(k + 1) * TT],
                                                   start=(k == 0), stop=(k == NFF - 1))
                        ins.then_inc(sPE, 1)
                        nc.vector.wait_ge(sPE, ve_wp.h)
                        ve_wp.bump(1)
                        if m == 0:
                            nc.vector.wait_ge(sS, ve_ws.h)
                        s0, s1 = m * TT, (m + 1) * TT
                        nc.vector.tensor_scalar(yfv[:, s0:s1], pb[:], s3c[:, m:m + 1], None, ALU.mult)
                        nc.vector.tensor_add(yfv[:, s0:s1], yfv[:, s0:s1], rtv[:, s0:s1]).then_inc(sV, 2)
                    nc.sync.wait_ge(sV, sy_wv2.h)
                    nc.sync.dma_start(out=outu[:, :, bass.DynSlice(sy_oo.v, TTB)],
                                      in_=yfv.bitcast(U8)).then_inc(sS, 16)
                    sy_wv.bump(16); sy_wv2.bump(16)
                    sy_oin.bump(TT * 2); sy_or.bump(TTB); sy_oo.bump(TTB)
                    te_wl.bump(32); te_wv0.bump(16); te_wv.bump(4)
                    ve_ws.bump(16)
            nc.sync.wait_ge(sS, 16 * (PB_S + NTILE))
            rs.close()


def stage_l1_qkv(g, i):
    nc = g.nc
    TTB = TT * 4
    il = (i >= 1)
    srcu = _fmu(g.zPre) if il else _fmu(g.hbf)
    zru = _fmu(g.zR)
    qbu = _fmu(g.QB)
    R_PE = 25 if il else 24
    R_A = 56 if il else 48
    R_V = 94 if il else 70
    n_ld = 3 if il else 1
    n_st = 2 if il else 1
    A_CAST = 8 if il else 0
    V_PROJ = 24 if il else 0
    ILF = 1 if il else 0
    with nc.reset_on_exit():
        with ExitStack() as st:
            W = {}
            sW = st.enter_context(nc.semaphore("sW"))
            for nm, hsrc in (("q", g.in_Wq), ("k", g.in_Wk), ("v", g.in_Wv)):
                W[nm] = st.enter_context(nc.sbuf_tensor(g.u(f"W{nm}{i}"), [128, NCH * D], BF16))
                for k in range(NCH):
                    nc.gpsimd.dma_start(out=W[nm][:, k * D:(k + 1) * D],
                                        in_=hsrc[i, k * 128:(k + 1) * 128, :]).then_inc(sW, 16)
            nc.tensor.wait_ge(sW, 16 * 3 * NCH)
            xt = st.enter_context(nc.sbuf_tensor(g.u(f"xtA{i}"), [128, 2 * NCH * TT], F32 if il else BF16))
            abt = st.enter_context(nc.sbuf_tensor(g.u(f"abA{i}"), [1, 4 * TT], F32)) if il else None
            zb = st.enter_context(nc.sbuf_tensor(g.u(f"zbA{i}"), [128, 2 * NCH * TT], BF16)) if il else None
            zf = st.enter_context(nc.sbuf_tensor(g.u(f"zfA{i}"), [128, 2 * NCH * TT], F32)) if il else None
            qb = st.enter_context(nc.sbuf_tensor(g.u(f"qbA{i}"), [128, 2 * NCH * TT], BF16))
            kf = st.enter_context(nc.sbuf_tensor(g.u(f"kfA{i}"), [128, NCH * TT], F32))
            vf = st.enter_context(nc.sbuf_tensor(g.u(f"vfA{i}"), [128, NCH * TT], F32))
            fa = st.enter_context(nc.sbuf_tensor(g.u(f"faA{i}"), [128, 2 * TT], F32))
            fb = st.enter_context(nc.sbuf_tensor(g.u(f"fbA{i}"), [128, 2 * TT], F32))
            prod = st.enter_context(nc.sbuf_tensor(g.u(f"prA{i}"), [128, 2 * TT], F32))
            rsl = st.enter_context(nc.sbuf_tensor(g.u(f"rsA{i}"), [128, 4], F32))
            ksl = st.enter_context(nc.sbuf_tensor(g.u(f"ksA{i}"), [128, 4], F32))
            pq = [st.enter_context(nc.psum_tensor(g.u(f"pA{j}{i}"), [128, TT], F32)) for j in range(4)]
            if il:
                pA = [st.enter_context(nc.psum_tensor(g.u(f"pBA{j}{i}"), [128, TT], F32)) for j in range(2)]
                pB = [st.enter_context(nc.psum_tensor(g.u(f"pBB{j}{i}"), [128, TT], F32)) for j in range(2)]
            sL = st.enter_context(nc.semaphore("sL"))
            sPE = st.enter_context(nc.semaphore("sPE"))
            sA = st.enter_context(nc.semaphore("sA"))
            sV = st.enter_context(nc.semaphore("sV"))
            sS = st.enter_context(nc.semaphore("sS"))
            rs = RegSet(g)
            nc.vector.memset(g.KVc[:], 0.0)
            nc.vector.memset(g.KSc[:], 0.0)
            PB_PE, PB_A, PB_V, PB_S = R_PE, R_A, R_V, n_st
            nc.tensor.sem_inc(sPE, PB_PE)
            nc.scalar.sem_inc(sA, PB_A)
            nc.vector.sem_inc(sV, PB_V)
            nc.sync.sem_inc(sS, 16 * PB_S)
            if il:
                sy_wld = rs.r(nc.sync, PB_V)
            else:
                sy_wld = rs.r(nc.sync, PB_PE)
            sy_wv2 = rs.r(nc.sync, PB_V + V_PROJ) if il else None
            sy_wq = rs.r(nc.sync, PB_V + V_PROJ + 8)
            sy_oin = rs.r(nc.sync, 0, hi=(NTILE - 1) * (TTB if il else TT * 2))
            sy_oab = rs.r(nc.sync, 0, hi=(NTILE - 1) * TTB) if il else None
            sy_oab2 = rs.r(nc.sync, 4 * T, hi=4 * T + (NTILE - 1) * TTB) if il else None
            sy_ozr = rs.r(nc.sync, 0, hi=(NTILE - 1) * TTB) if il else None
            sy_oq = rs.r(nc.sync, 0, hi=(NTILE - 1) * TT * 2)
            te_wl = rs.r(nc.tensor, 16 * n_ld)
            te_wv0 = rs.r(nc.tensor, PB_V)
            te_wa0 = rs.r(nc.tensor, PB_A)
            te_wzb = rs.r(nc.tensor, PB_A + A_CAST) if il else None
            te_wevA = rs.r(nc.tensor, PB_A + A_CAST + 3)
            te_wevV = rs.r(nc.tensor, PB_V + V_PROJ + 17)
            sc_wpe0 = rs.r(nc.scalar, PB_PE)
            sc_wv1 = rs.r(nc.scalar, PB_V + V_PROJ) if il else None
            sc_wpe = rs.r(nc.scalar, PB_PE + ILF + 1)
            sc_wfa = rs.r(nc.scalar, PB_V + V_PROJ - 1)
            ve_wl = rs.r(nc.vector, 16 * n_ld)
            ve_ws = rs.r(nc.vector, 16 * PB_S)
            ve_wpeb = rs.r(nc.vector, PB_PE + 1) if il else None
            ve_wa = rs.r(nc.vector, PB_A + A_CAST + 3)
            ve_wpev = rs.r(nc.vector, PB_PE + ILF + 17)
            with fori_clean(nc, NTILE // 2, [ET.SP, ET.Activation, ET.PE, ET.DVE]):
                for par in range(2):
                    o = par * NCH * TT
                    xtv = xt[:, o:o + NCH * TT]
                    qbv = qb[:, o:o + NCH * TT]
                    nc.sync.wait_ge(sV if il else sPE, sy_wld.h)
                    if il:
                        nc.sync.dma_start(out=xtv.bitcast(U8),
                                          in_=srcu[:, :, bass.DynSlice(sy_oin.v, TTB)]).then_inc(sL, 16)
                        nc.sync.dma_start(out=abt[0:1, par * 2 * TT:par * 2 * TT + TT].bitcast(U8),
                                          in_=g.rowAB.bitcast(U8)[0:1, bass.DynSlice(sy_oab.v, TTB)]).then_inc(sL, 16)
                        nc.sync.dma_start(out=abt[0:1, par * 2 * TT + TT:par * 2 * TT + 2 * TT].bitcast(U8),
                                          in_=g.rowAB.bitcast(U8)[0:1, bass.DynSlice(sy_oab2.v, TTB)]).then_inc(sL, 16)
                    else:
                        nc.sync.dma_start(out=xtv.bitcast(U8),
                                          in_=srcu[:, :, bass.DynSlice(sy_oin.v, TT * 2)]).then_inc(sL, 16)
                    if il:
                        zbv = zb[:, o:o + NCH * TT]
                        zfv = zf[:, o:o + NCH * TT]
                        nc.tensor.wait_ge(sL, te_wl.h)
                        nc.tensor.wait_ge(sV, te_wv0.h)
                        nc.tensor.matmul(pA[par][:], g.onesf[:], abt[0:1, par * 2 * TT:par * 2 * TT + TT],
                                         start=True, stop=True)
                        nc.tensor.matmul(pB[par][:], g.onesf[:], abt[0:1, par * 2 * TT + TT:par * 2 * TT + 2 * TT],
                                         start=True, stop=True).then_inc(sPE, 1)
                        nc.vector.wait_ge(sL, ve_wl.h)
                        nc.vector.wait_ge(sS, ve_ws.h)
                        nc.vector.wait_ge(sPE, ve_wpeb.h)
                        for m in range(NCH):
                            s0, s1 = m * TT, (m + 1) * TT
                            nc.vector.tensor_mul(zfv[:, s0:s1], xtv[:, s0:s1], pA[par][:])
                            ins = nc.vector.tensor_add(zfv[:, s0:s1], zfv[:, s0:s1], pB[par][:])
                        ins.then_inc(sV, 16)
                        for m in range(NCH):
                            s0, s1 = m * TT, (m + 1) * TT
                            ins = nc.vector.tensor_scalar(zfv[:, s0:s1], zfv[:, s0:s1],
                                                          g.l2s_c[:, (i - 1) * 8 + m:(i - 1) * 8 + m + 1],
                                                          g.l2b_c[:, (i - 1) * 8 + m:(i - 1) * 8 + m + 1],
                                                          ALU.mult, ALU.add)
                        ins.then_inc(sV, 8)
                        nc.scalar.wait_ge(sPE, sc_wpe0.h)
                        nc.scalar.wait_ge(sV, sc_wv1.h)
                        for m in range(NCH):
                            ins = nc.scalar.activation(zbv[:, m * TT:(m + 1) * TT],
                                                       zfv[:, m * TT:(m + 1) * TT], AF.Copy)
                        ins.then_inc(sA, 8)
                        nc.sync.wait_ge(sV, sy_wv2.h)
                        nc.sync.dma_start(out=zru[:, :, bass.DynSlice(sy_ozr.v, TTB)],
                                          in_=zfv.bitcast(U8)).then_inc(sS, 16)
                        zin = zbv
                    else:
                        nc.tensor.wait_ge(sL, te_wl.h)
                        nc.tensor.wait_ge(sV, te_wv0.h)
                        nc.vector.wait_ge(sS, ve_ws.h)
                        zin = xtv
                    nc.tensor.wait_ge(sA, te_wa0.h)
                    if il:
                        nc.tensor.wait_ge(sA, te_wzb.h)
                    for gq in range(24):
                        proj, m = gq // 8, gq % 8
                        Wt = (W["q"], W["k"], W["v"])[proj]
                        pb = pq[2 * par + (gq % 2)]
                        if gq >= 2:
                            if gq - 2 < 16:
                                nc.tensor.wait_ge(sA, te_wevA.h)
                                te_wevA.bump(3)
                            else:
                                nc.tensor.wait_ge(sV, te_wevV.h)
                                te_wevV.bump(1)
                        for k in range(NCH):
                            ins = nc.tensor.matmul(pb[:], Wt[:, k * D + m * 128:k * D + m * 128 + 128],
                                                   zin[:, k * TT:(k + 1) * TT],
                                                   start=(k == 0), stop=(k == NCH - 1))
                        ins.then_inc(sPE, 1)
                        bci = i * 8 + m
                        if proj < 2:
                            bcolv = g.bq_c if proj == 0 else g.bk_c
                            nbcolv = g.nbq_c if proj == 0 else g.nbk_c
                            sl = (m % 2) * TT
                            nc.scalar.wait_ge(sPE, sc_wpe.h)
                            sc_wpe.bump(1)
                            nc.scalar.wait_ge(sV, sc_wfa.h)
                            sc_wfa.bump(1)
                            nc.scalar.activation(fa[:, sl:sl + TT], pb[:], AF.Relu,
                                                 scale=-1.0, bias=nbcolv[:, bci:bci + 1])
                            nc.scalar.activation(fa[:, sl:sl + TT], fa[:, sl:sl + TT], AF.Exp, scale=-1.0)
                            ins = nc.scalar.activation(fb[:, sl:sl + TT], pb[:], AF.Relu,
                                                       bias=bcolv[:, bci:bci + 1])
                            ins.then_inc(sA, 3)
                            nc.vector.wait_ge(sA, ve_wa.h)
                            ve_wa.bump(3)
                            dst = qbv if proj == 0 else kf
                            ins = nc.vector.tensor_add(dst[:, m * TT:(m + 1) * TT],
                                                       fa[:, sl:sl + TT], fb[:, sl:sl + TT])
                            ins.then_inc(sV, 1)
                        else:
                            nc.vector.wait_ge(sPE, ve_wpev.h)
                            ve_wpev.bump(1)
                            ins = nc.vector.tensor_scalar(vf[:, m * TT:(m + 1) * TT], pb[:],
                                                          g.bv_c[:, bci:bci + 1], None, ALU.add)
                            ins.then_inc(sV, 1)
                        if gq == 7:
                            nc.sync.wait_ge(sV, sy_wq.h)
                            nc.sync.dma_start(out=qbu[:, :, bass.DynSlice(sy_oq.v, TT * 2)],
                                              in_=qbv.bitcast(U8)).then_inc(sS, 16)
                    for m in range(NCH):
                        nc.vector.tensor_mul(prod[:, (m % 2) * TT:(m % 2 + 1) * TT],
                                             kf[:, m * TT:(m + 1) * TT], vf[:, m * TT:(m + 1) * TT])
                        nc.vector.reduce_sum(rsl[:, m % 4:m % 4 + 1],
                                             prod[:, (m % 2) * TT:(m % 2 + 1) * TT], axis=AX.X)
                        nc.vector.reduce_sum(ksl[:, m % 4:m % 4 + 1],
                                             kf[:, m * TT:(m + 1) * TT], axis=AX.X)
                        if m >= 2:
                            nc.vector.tensor_add(g.KVc[:, m - 2:m - 1], g.KVc[:, m - 2:m - 1],
                                                 rsl[:, (m - 2) % 4:(m - 2) % 4 + 1])
                            ins = nc.vector.tensor_add(g.KSc[:, m - 2:m - 1], g.KSc[:, m - 2:m - 1],
                                                       ksl[:, (m - 2) % 4:(m - 2) % 4 + 1])
                        else:
                            nc.vector.engine_nop()
                            ins = nc.vector.engine_nop()
                        ins.then_inc(sV, 5)
                    nc.vector.engine_nop()
                    nc.vector.engine_nop()
                    nc.vector.tensor_add(g.KVc[:, 6:7], g.KVc[:, 6:7], rsl[:, 2:3])
                    nc.vector.tensor_add(g.KSc[:, 6:7], g.KSc[:, 6:7], ksl[:, 2:3])
                    nc.vector.tensor_add(g.KVc[:, 7:8], g.KVc[:, 7:8], rsl[:, 3:4])
                    ins = nc.vector.tensor_add(g.KSc[:, 7:8], g.KSc[:, 7:8], ksl[:, 3:4])
                    ins.then_inc(sV, 6)
                    sy_wld.bump(R_V if il else R_PE)
                    if il:
                        sy_wv2.bump(R_V)
                        sy_oab.bump(TTB); sy_oab2.bump(TTB); sy_ozr.bump(TTB)
                    sy_wq.bump(R_V)
                    sy_oin.bump(TTB if il else TT * 2)
                    sy_oq.bump(TT * 2)
                    te_wl.bump(16 * n_ld); te_wv0.bump(R_V); te_wa0.bump(R_A)
                    if il:
                        te_wzb.bump(R_A)
                        sc_wv1.bump(R_V)
                        ve_wpeb.bump(R_PE)
                    te_wevA.bump(R_A - 48)
                    te_wevV.bump(R_V - 6)
                    sc_wpe0.bump(R_PE)
                    sc_wpe.bump(R_PE - 16)
                    sc_wfa.bump(R_V - 16)
                    ve_wl.bump(16 * n_ld)
                    ve_ws.bump(16 * n_st)
                    ve_wa.bump(R_A - 48)
                    ve_wpev.bump(R_PE - 8)
            nc.sync.wait_ge(sS, 16 * (PB_S + n_st * NTILE))
            rs.close()


def stage_l1_B1(g, i):
    nc = g.nc
    TTB = TT * 4
    qbu = _fmu(g.QB)
    zru = _fmu(g.zR)
    upu = _fmu(g.uPre)
    R_PE, R_A, R_V = 10, 24, 42
    with nc.reset_on_exit():
        with ExitStack() as st:
            sW = st.enter_context(nc.semaphore("sW"))
            Wo = st.enter_context(nc.sbuf_tensor(g.u(f"WoB{i}"), [128, NCH * D], BF16))
            for k in range(NCH):
                nc.gpsimd.dma_start(out=Wo[:, k * D:(k + 1) * D],
                                    in_=g.in_Wo[i, k * 128:(k + 1) * 128, :]).then_inc(sW, 16)
            nc.tensor.wait_ge(sW, 16 * NCH)
            qt = st.enter_context(nc.sbuf_tensor(g.u(f"qtB{i}"), [128, 2 * NCH * TT], BF16))
            zt = st.enter_context(nc.sbuf_tensor(g.u(f"ztB{i}"), [128, 2 * NCH * TT], F32))
            t1 = st.enter_context(nc.sbuf_tensor(g.u(f"t1B{i}"), [128, NCH * TT], F32))
            vb = st.enter_context(nc.sbuf_tensor(g.u(f"vbB{i}"), [128, 2 * NCH * TT], BF16))
            ax = st.enter_context(nc.sbuf_tensor(g.u(f"axB{i}"), [128, 2 * TT], F32))
            yf = st.enter_context(nc.sbuf_tensor(g.u(f"yfB{i}"), [128, 2 * NCH * TT], F32))
            xb = st.enter_context(nc.sbuf_tensor(g.u(f"xbB{i}"), [128, NCH * TT], BF16))
            xq = st.enter_context(nc.sbuf_tensor(g.u(f"xqB{i}"), [128, NCH * TT], BF16))
            srow = st.enter_context(nc.sbuf_tensor(g.u(f"srB{i}"), [1, 4 * TT], F32))
            pq = [st.enter_context(nc.psum_tensor(g.u(f"pB{j}{i}"), [128, TT], F32)) for j in range(4)]
            ps = [st.enter_context(nc.psum_tensor(g.u(f"psB{j}{i}"), [1, TT], F32)) for j in range(4)]
            sL = st.enter_context(nc.semaphore("sL"))
            sPE = st.enter_context(nc.semaphore("sPE"))
            sA = st.enter_context(nc.semaphore("sA"))
            sV = st.enter_context(nc.semaphore("sV"))
            sS = st.enter_context(nc.semaphore("sS"))
            rs = RegSet(g)
            PB_PE, PB_A, PB_V, PB_S = 2 * R_PE, R_A, R_V, 3
            nc.tensor.sem_inc(sPE, PB_PE)
            nc.scalar.sem_inc(sA, PB_A)
            nc.vector.sem_inc(sV, PB_V)
            nc.sync.sem_inc(sS, 16 * PB_S)
            sy_wld = rs.r(nc.sync, PB_V)
            ve_wpe0 = rs.r(nc.vector, R_PE)
            sy_wu = rs.r(nc.sync, PB_V + 40)
            sy_ws2 = rs.r(nc.sync, PB_V + 42)
            sy_oq = rs.r(nc.sync, 0, hi=(NTILE - 1) * TT * 2)
            sy_oz = rs.r(nc.sync, 0, hi=(NTILE - 1) * TTB)
            sy_ou = rs.r(nc.sync, 0, hi=(NTILE - 1) * TTB)
            sy_os1 = rs.r(nc.sync, 0, hi=(NTILE - 1) * TTB)
            sy_os2 = rs.r(nc.sync, 4 * T, hi=4 * T + (NTILE - 1) * TTB)
            te_wV = rs.r(nc.tensor, PB_V + 25)
            te_wa0 = rs.r(nc.tensor, PB_A)
            te_wua = rs.r(nc.tensor, PB_A + 2)
            te_wsq = rs.r(nc.tensor, PB_A + 24)
            te_wv1 = rs.r(nc.tensor, PB_V)
            sc_wpe = rs.r(nc.scalar, PB_PE + 1)
            sc_waux = rs.r(nc.scalar, PB_V + 31)
            sc_wua = rs.r(nc.scalar, PB_V + 33)
            ve_wl = rs.r(nc.vector, 32)
            ve_ws = rs.r(nc.vector, 16 * PB_S)
            ve_wg = rs.r(nc.vector, PB_A + 1)
            ve_wst = rs.r(nc.vector, PB_PE + R_PE)
            stat_u = g.statraw.bitcast(U8)
            with fori_clean(nc, NTILE // 2, [ET.SP, ET.Activation, ET.PE, ET.DVE]):
                for par in range(2):
                    o = par * NCH * TT
                    qtv = qt[:, o:o + NCH * TT]
                    ztv = zt[:, o:o + NCH * TT]
                    yfv = yf[:, o:o + NCH * TT]
                    nc.sync.wait_ge(sV, sy_wld.h)
                    nc.sync.dma_start(out=qtv.bitcast(U8),
                                      in_=qbu[:, :, bass.DynSlice(sy_oq.v, TT * 2)]).then_inc(sL, 16)
                    nc.sync.dma_start(out=ztv.bitcast(U8),
                                      in_=zru[:, :, bass.DynSlice(sy_oz.v, TTB)]).then_inc(sL, 16)
                    vbv = vb[:, o:o + NCH * TT]
                    nc.vector.wait_ge(sL, ve_wl.h)
                    nc.vector.wait_ge(sS, ve_ws.h)
                    nc.vector.wait_ge(sPE, ve_wpe0.h)
                    for m in range(NCH):
                        s0, s1 = m * TT, (m + 1) * TT
                        ins = nc.vector.tensor_scalar(t1[:, s0:s1], qtv[:, s0:s1],
                                                      g.KSc[:, m:m + 1], 1e-6, ALU.mult, ALU.add)
                    ins.then_inc(sV, 8)
                    for m in range(NCH):
                        s0, s1 = m * TT, (m + 1) * TT
                        ins = nc.vector.reciprocal(t1[:, s0:s1], t1[:, s0:s1])
                    ins.then_inc(sV, 8)
                    for m in range(NCH):
                        s0, s1 = m * TT, (m + 1) * TT
                        ins = nc.vector.tensor_mul(t1[:, s0:s1], t1[:, s0:s1], qtv[:, s0:s1])
                    ins.then_inc(sV, 8)
                    for m in range(NCH):
                        s0, s1 = m * TT, (m + 1) * TT
                        ins = nc.vector.tensor_scalar(vbv[:, s0:s1], t1[:, s0:s1],
                                                      g.KVc[:, m:m + 1], None, ALU.mult)
                    ins.then_inc(sV, 8)
                    for m in range(NCH):
                        pb = pq[2 * par + (m % 2)]
                        nc.tensor.wait_ge(sV, te_wV.h)
                        te_wV.bump(1)
                        if m == 0:
                            nc.tensor.wait_ge(sA, te_wa0.h)
                        if m >= 2:
                            nc.tensor.wait_ge(sA, te_wua.h)
                            te_wua.bump(3)
                        for k in range(NCH):
                            ins = nc.tensor.matmul(pb[:], Wo[:, k * D + m * 128:k * D + m * 128 + 128],
                                                   vbv[:, k * TT:(k + 1) * TT],
                                                   start=(k == 0), stop=(k == NCH - 1))
                        ins.then_inc(sPE, 1)
                        bci = i * 8 + m
                        sl = (m % 2) * TT
                        nc.scalar.wait_ge(sPE, sc_wpe.h)
                        sc_wpe.bump(1)
                        nc.scalar.wait_ge(sV, sc_waux.h)
                        sc_waux.bump(1)
                        nc.scalar.activation(ax[:, sl:sl + TT], pb[:], AF.Gelu_apprx_tanh,
                                             bias=g.bo_c[:, bci:bci + 1]).then_inc(sA, 1)
                        s0, s1 = m * TT, (m + 1) * TT
                        nc.vector.wait_ge(sA, ve_wg.h)
                        ve_wg.bump(3)
                        nc.vector.tensor_add(yfv[:, s0:s1], ax[:, sl:sl + TT], ztv[:, s0:s1]).then_inc(sV, 1)
                        nc.scalar.wait_ge(sV, sc_wua.h)
                        sc_wua.bump(1)
                        nc.scalar.activation(xb[:, s0:s1], yfv[:, s0:s1], AF.Copy)
                        nc.scalar.activation(xq[:, s0:s1], yfv[:, s0:s1], AF.Square).then_inc(sA, 2)
                    pS, pQ = ps[2 * par], ps[2 * par + 1]
                    nc.tensor.wait_ge(sA, te_wsq.h)
                    nc.tensor.wait_ge(sV, te_wv1.h)
                    for m in range(NCH):
                        nc.tensor.matmul(pS[:], g.ones_col_bf[:], xb[:, m * TT:(m + 1) * TT],
                                         start=(m == 0), stop=(m == NCH - 1))
                    nc.tensor.nop().then_inc(sPE, 1)
                    for m in range(NCH):
                        ins = nc.tensor.matmul(pQ[:], g.ones_col_bf[:], xq[:, m * TT:(m + 1) * TT],
                                               start=(m == 0), stop=(m == NCH - 1))
                    ins.then_inc(sPE, 1)
                    nc.vector.wait_ge(sPE, ve_wst.h)
                    nc.vector.tensor_copy(srow[0:1, par * TT:(par + 1) * TT], pS[:])
                    nc.vector.tensor_copy(srow[0:1, (2 + par) * TT:(3 + par) * TT], pQ[:]).then_inc(sV, 2)
                    nc.sync.wait_ge(sV, sy_wu.h)
                    nc.sync.dma_start(out=upu[:, :, bass.DynSlice(sy_ou.v, TTB)],
                                      in_=yfv.bitcast(U8)).then_inc(sS, 16)
                    nc.sync.wait_ge(sV, sy_ws2.h)
                    nc.sync.dma_start(out=stat_u[0:1, bass.DynSlice(sy_os1.v, TTB)],
                                      in_=srow[0:1, par * TT:(par + 1) * TT].bitcast(U8)).then_inc(sS, 16)
                    nc.sync.dma_start(out=stat_u[0:1, bass.DynSlice(sy_os2.v, TTB)],
                                      in_=srow[0:1, (2 + par) * TT:(3 + par) * TT].bitcast(U8)).then_inc(sS, 16)
                    sy_wld.bump(R_V); sy_wu.bump(R_V); sy_ws2.bump(R_V)
                    sy_oq.bump(TT * 2); sy_oz.bump(TTB); sy_ou.bump(TTB)
                    sy_os1.bump(TTB); sy_os2.bump(TTB)
                    te_wV.bump(R_V - 8); te_wa0.bump(R_A); te_wua.bump(R_A - 18)
                    te_wsq.bump(R_A); te_wv1.bump(R_V)
                    sc_wpe.bump(R_PE - 8); sc_waux.bump(R_V - 8); sc_wua.bump(R_V - 8)
                    ve_wl.bump(32); ve_ws.bump(16 * PB_S); ve_wpe0.bump(R_PE)
                    ve_wg.bump(R_A - 24); ve_wst.bump(R_PE)
            nc.sync.wait_ge(sS, 16 * (PB_S + 3 * NTILE))
            rs.close()


def stage_l1_B2(g, i):
    nc = g.nc
    TB = 256
    NT2 = T // TB
    TBB = TB * 4
    upu = _fmu(g.uPre)
    zpu = _fmu(g.zPre)
    R_PE, R_A, R_V = 43, 64, 34
    with nc.reset_on_exit():
        with ExitStack() as st:
            sW = st.enter_context(nc.semaphore("sW"))
            W1 = st.enter_context(nc.sbuf_tensor(g.u(f"W1B{i}"), [128, NCH * FF], BF16))
            W2 = st.enter_context(nc.sbuf_tensor(g.u(f"W2B{i}"), [128, NFF * D], BF16))
            for k in range(NCH):
                nc.gpsimd.dma_start(out=W1[:, k * FF:(k + 1) * FF],
                                    in_=g.in_W1[i, k * 128:(k + 1) * 128, :]).then_inc(sW, 16)
            for k in range(NFF):
                nc.gpsimd.dma_start(out=W2[:, k * D:(k + 1) * D],
                                    in_=g.in_W2[i, k * 128:(k + 1) * 128, :]).then_inc(sW, 16)
            nc.tensor.wait_ge(sW, 16 * (NCH + NFF))
            ut = st.enter_context(nc.sbuf_tensor(g.u(f"utB{i}"), [128, NCH * TB], F32))
            abt = st.enter_context(nc.sbuf_tensor(g.u(f"abB{i}"), [1, 4 * TB], F32))
            aux = st.enter_context(nc.sbuf_tensor(g.u(f"auB{i}"), [128, NCH * TB], F32))
            uf = aux
            ub = st.enter_context(nc.sbuf_tensor(g.u(f"ubB{i}"), [128, NCH * TB], BF16))
            ht = st.enter_context(nc.sbuf_tensor(g.u(f"htB{i}"), [128, NFF * TB], BF16))
            ax2 = st.enter_context(nc.sbuf_tensor(g.u(f"axB2{i}"), [128, 2 * TB], F32))
            yf = st.enter_context(nc.sbuf_tensor(g.u(f"yfB2{i}"), [128, 2 * NCH * TB], F32))
            xb = st.enter_context(nc.sbuf_tensor(g.u(f"xbB2{i}"), [128, NCH * TB], BF16))
            xq = st.enter_context(nc.sbuf_tensor(g.u(f"xqB2{i}"), [128, NCH * TB], BF16))
            srow = st.enter_context(nc.sbuf_tensor(g.u(f"srB2{i}"), [1, 4 * TB], F32))
            pq = [st.enter_context(nc.psum_tensor(g.u(f"pB2{j}{i}"), [128, TB], F32)) for j in range(4)]
            pA = st.enter_context(nc.psum_tensor(g.u(f"pAB2{i}"), [128, TB], F32))
            pB = st.enter_context(nc.psum_tensor(g.u(f"pBB2{i}"), [128, TB], F32))
            ps = [st.enter_context(nc.psum_tensor(g.u(f"psB2{j}{i}"), [1, TB], F32)) for j in range(2)]
            sL = st.enter_context(nc.semaphore("sL"))
            sPE = st.enter_context(nc.semaphore("sPE"))
            sA = st.enter_context(nc.semaphore("sA"))
            sV = st.enter_context(nc.semaphore("sV"))
            sS = st.enter_context(nc.semaphore("sS"))
            rs = RegSet(g)
            PB_PE, PB_A, PB_V, PB_S = R_PE, R_A, R_V, 3
            nc.tensor.sem_inc(sPE, PB_PE)
            nc.scalar.sem_inc(sA, PB_A)
            nc.vector.sem_inc(sV, PB_V)
            nc.sync.sem_inc(sS, 16 * PB_S)
            sy_wld = rs.r(nc.sync, PB_V)
            sy_wz = rs.r(nc.sync, PB_V + 32)
            sy_ws2 = rs.r(nc.sync, PB_V + 34)
            sy_ou = rs.r(nc.sync, 0, hi=(NT2 - 1) * TBB)
            sy_oab = rs.r(nc.sync, 0, hi=(NT2 - 1) * TBB)
            sy_oab2 = rs.r(nc.sync, 4 * T, hi=4 * T + (NT2 - 1) * TBB)
            sy_oz = rs.r(nc.sync, 0, hi=(NT2 - 1) * TBB)
            sy_os1 = rs.r(nc.sync, 0, hi=(NT2 - 1) * TBB)
            sy_os2 = rs.r(nc.sync, 4 * T, hi=4 * T + (NT2 - 1) * TBB)
            te_wl = rs.r(nc.tensor, 16 * 3)
            te_wv0 = rs.r(nc.tensor, PB_V)
            te_wa0 = rs.r(nc.tensor, PB_A)
            te_wub = rs.r(nc.tensor, PB_A + 8)
            te_wevh = rs.r(nc.tensor, PB_A + 9)
            te_wev2 = rs.r(nc.tensor, PB_A + 39)
            te_wsq = rs.r(nc.tensor, PB_A + 64)
            sc_wvln = rs.r(nc.scalar, PB_V + 24)
            sc_wpe1 = rs.r(nc.scalar, PB_PE + 2)
            sc_wpe2 = rs.r(nc.scalar, PB_PE + 34)
            sc_wax2 = rs.r(nc.scalar, PB_V + 23)
            sc_wua = rs.r(nc.scalar, PB_V + 25)
            ve_wl = rs.r(nc.vector, 16 * 1)
            ve_ws = rs.r(nc.vector, 16 * PB_S)
            ve_wpeb = rs.r(nc.vector, PB_PE + 1)
            ve_wg2 = rs.r(nc.vector, PB_A + 41)
            ve_wst = rs.r(nc.vector, PB_PE + 43)
            stat_u = g.statraw.bitcast(U8)
            with fori_clean(nc, NT2 // 2, [ET.SP, ET.Activation, ET.PE, ET.DVE]):
                for par in range(2):
                    o = par * NCH * TB
                    utv = ut[:, 0:NCH * TB]
                    yfv = yf[:, o:o + NCH * TB]
                    nc.sync.wait_ge(sV, sy_wld.h)
                    nc.sync.dma_start(out=utv.bitcast(U8),
                                      in_=upu[:, :, bass.DynSlice(sy_ou.v, TBB)]).then_inc(sL, 16)
                    nc.sync.dma_start(out=abt[0:1, par * 2 * TB:par * 2 * TB + TB].bitcast(U8),
                                      in_=g.rowAB.bitcast(U8)[0:1, bass.DynSlice(sy_oab.v, TBB)]).then_inc(sL, 16)
                    nc.sync.dma_start(out=abt[0:1, par * 2 * TB + TB:par * 2 * TB + 2 * TB].bitcast(U8),
                                      in_=g.rowAB.bitcast(U8)[0:1, bass.DynSlice(sy_oab2.v, TBB)]).then_inc(sL, 16)
                    nc.tensor.wait_ge(sL, te_wl.h)
                    nc.tensor.wait_ge(sV, te_wv0.h)
                    nc.tensor.wait_ge(sA, te_wa0.h)
                    nc.tensor.matmul(pA[:], g.onesf[:], abt[0:1, par * 2 * TB:par * 2 * TB + TB],
                                     start=True, stop=True)
                    nc.tensor.matmul(pB[:], g.onesf[:], abt[0:1, par * 2 * TB + TB:par * 2 * TB + 2 * TB],
                                     start=True, stop=True).then_inc(sPE, 1)
                    nc.vector.wait_ge(sL, ve_wl.h)
                    nc.vector.wait_ge(sS, ve_ws.h)
                    nc.vector.wait_ge(sPE, ve_wpeb.h)
                    for m in range(NCH):
                        s0, s1 = m * TB, (m + 1) * TB
                        nc.vector.tensor_mul(aux[:, s0:s1], utv[:, s0:s1], pA[:])
                        ins = nc.vector.tensor_add(aux[:, s0:s1], aux[:, s0:s1], pB[:])
                    ins.then_inc(sV, 16)
                    for m in range(NCH):
                        s0, s1 = m * TB, (m + 1) * TB
                        ins = nc.vector.tensor_scalar(uf[:, s0:s1], aux[:, s0:s1],
                                                      g.l1s_c[:, i * 8 + m:i * 8 + m + 1],
                                                      g.l1b_c[:, i * 8 + m:i * 8 + m + 1],
                                                      ALU.mult, ALU.add)
                    ins.then_inc(sV, 8)
                    nc.scalar.wait_ge(sV, sc_wvln.h)
                    for m in range(NCH):
                        ins = nc.scalar.activation(ub[:, m * TB:(m + 1) * TB],
                                                   uf[:, m * TB:(m + 1) * TB], AF.Copy)
                    ins.then_inc(sA, 8)
                    nc.tensor.wait_ge(sA, te_wub.h)
                    for m in range(NFF):
                        pb = pq[2 * par + (m % 2)]
                        if m >= 2:
                            nc.tensor.wait_ge(sA, te_wevh.h)
                            te_wevh.bump(1)
                        for k in range(NCH):
                            ins = nc.tensor.matmul(pb[:], W1[:, k * FF + m * 128:k * FF + m * 128 + 128],
                                                   ub[:, k * TB:(k + 1) * TB],
                                                   start=(k == 0), stop=(k == NCH - 1))
                        ins.then_inc(sPE, 1)
                        nc.scalar.wait_ge(sPE, sc_wpe1.h)
                        sc_wpe1.bump(1)
                        nc.scalar.activation(ht[:, m * TB:(m + 1) * TB], pb[:], AF.Gelu_apprx_tanh,
                                             bias=g.b1_c[:, i * 32 + m:i * 32 + m + 1]).then_inc(sA, 1)
                    for m in range(NCH):
                        pb = pq[2 * par + (m % 2)]
                        nc.tensor.wait_ge(sA, te_wev2.h)
                        te_wev2.bump(1)
                        for k in range(NFF):
                            ins = nc.tensor.matmul(pb[:], W2[:, k * D + m * 128:k * D + m * 128 + 128],
                                                   ht[:, k * TB:(k + 1) * TB],
                                                   start=(k == 0), stop=(k == NFF - 1))
                        ins.then_inc(sPE, 1)
                        bci = i * 8 + m
                        sl = (m % 2) * TB
                        nc.scalar.wait_ge(sPE, sc_wpe2.h)
                        sc_wpe2.bump(1)
                        nc.scalar.wait_ge(sV, sc_wax2.h)
                        sc_wax2.bump(1)
                        nc.scalar.activation(ax2[:, sl:sl + TB], pb[:], AF.Gelu_apprx_tanh,
                                             bias=g.b2_c[:, bci:bci + 1]).then_inc(sA, 1)
                        s0, s1 = m * TB, (m + 1) * TB
                        nc.vector.wait_ge(sA, ve_wg2.h)
                        ve_wg2.bump(3)
                        nc.vector.tensor_add(yfv[:, s0:s1], ax2[:, sl:sl + TB], uf[:, s0:s1]).then_inc(sV, 1)
                        nc.scalar.wait_ge(sV, sc_wua.h)
                        sc_wua.bump(1)
                        nc.scalar.activation(xb[:, s0:s1], yfv[:, s0:s1], AF.Copy)
                        nc.scalar.activation(xq[:, s0:s1], yfv[:, s0:s1], AF.Square).then_inc(sA, 2)
                    pS, pQ = ps[0], ps[1]
                    nc.tensor.wait_ge(sA, te_wsq.h)
                    for m in range(NCH):
                        nc.tensor.matmul(pS[:], g.ones_col_bf[:], xb[:, m * TB:(m + 1) * TB],
                                         start=(m == 0), stop=(m == NCH - 1))
                    nc.tensor.nop().then_inc(sPE, 1)
                    for m in range(NCH):
                        ins = nc.tensor.matmul(pQ[:], g.ones_col_bf[:], xq[:, m * TB:(m + 1) * TB],
                                               start=(m == 0), stop=(m == NCH - 1))
                    ins.then_inc(sPE, 1)
                    nc.vector.wait_ge(sPE, ve_wst.h)
                    nc.vector.tensor_copy(srow[0:1, par * TB:(par + 1) * TB], pS[:])
                    nc.vector.tensor_copy(srow[0:1, (2 + par) * TB:(3 + par) * TB], pQ[:]).then_inc(sV, 2)
                    nc.sync.wait_ge(sV, sy_wz.h)
                    nc.sync.dma_start(out=zpu[:, :, bass.DynSlice(sy_oz.v, TBB)],
                                      in_=yfv.bitcast(U8)).then_inc(sS, 16)
                    nc.sync.wait_ge(sV, sy_ws2.h)
                    nc.sync.dma_start(out=stat_u[0:1, bass.DynSlice(sy_os1.v, TBB)],
                                      in_=srow[0:1, par * TB:(par + 1) * TB].bitcast(U8)).then_inc(sS, 16)
                    nc.sync.dma_start(out=stat_u[0:1, bass.DynSlice(sy_os2.v, TBB)],
                                      in_=srow[0:1, (2 + par) * TB:(3 + par) * TB].bitcast(U8)).then_inc(sS, 16)
                    sy_wld.bump(R_V); sy_wz.bump(R_V); sy_ws2.bump(R_V)
                    sy_ou.bump(TBB); sy_oab.bump(TBB); sy_oab2.bump(TBB)
                    sy_oz.bump(TBB); sy_os1.bump(TBB); sy_os2.bump(TBB)
                    te_wl.bump(48); te_wv0.bump(R_V); te_wa0.bump(R_A); te_wub.bump(R_A)
                    te_wevh.bump(R_A - 30); te_wev2.bump(R_A - 8); te_wsq.bump(R_A)
                    sc_wvln.bump(R_V)
                    sc_wpe1.bump(R_PE - 32); sc_wpe2.bump(R_PE - 8)
                    sc_wax2.bump(R_V - 8); sc_wua.bump(R_V - 8)
                    ve_wl.bump(16 * 3)
                    ve_ws.bump(16 * PB_S); ve_wpeb.bump(R_PE); ve_wg2.bump(R_A - 24)
                    ve_wst.bump(R_PE)
            nc.sync.wait_ge(sS, 16 * (PB_S + 3 * NT2))
            rs.close()


def extend_stages(g, stages):
    stages.append(lambda: stage_qkv_l0(g))
    stages.append(lambda: stage_attn_l0(g))
    stages.append(lambda: stage_wout_l0(g))
    stages.append(lambda: stage_stats(g, g.X1, "sf0"))
    stages.append(lambda: stage_ln_finalize(g, 1e-5))
    stages.append(lambda: stage_apply(g, g.X1, None, None, g.hbf, None, tag="af0"))
    stages.append(lambda: stage_ffn_a(g, g.in_f0_w1T, g.in_f0_w2T, g.f0_s1c, g.f0_s2c, g.hbf, g.ffh, g.handles["ffh"], "f0a"))
    stages.append(lambda: stage_ffn_mid(g, g.in_f0_h1T, g.f0_sh1c, g.ffh, g.ffg, g.handles["ffg"], "f0m1"))
    stages.append(lambda: stage_ffn_mid(g, g.in_f0_h2T, g.f0_sh2c, g.ffg, g.ffh, g.handles["ffh"], "f0m2"))
    stages.append(lambda: stage_ffn_c(g, g.in_f0_w3T, g.f0_s3c, g.ffh, g.X1, g.X2, "f0c"))
    stages.append(lambda: stage_stats(g, g.X2, "sl1"))
    stages.append(lambda: stage_ln_finalize(g, 1e-5))
    stages.append(lambda: stage_apply(g, g.X2, None, None, g.hbf, g.zR, tag="al1"))
    for i in range(NB):
        stages.append(lambda i=i: stage_l1_qkv(g, i))
        stages.append(lambda i=i: stage_l1_B1(g, i))
        stages.append(lambda: stage_ln_finalize(g, 1e-6))
        stages.append(lambda i=i: stage_l1_B2(g, i))
        stages.append(lambda: stage_ln_finalize(g, 1e-6))
    stages.append(lambda: stage_apply(g, g.zPre, g.l2s_c[:, 56:64], g.l2b_c[:, 56:64],
                                      None, g.X3, res_hbm=g.X2, tag="tl1"))
    stages.append(lambda: stage_stats(g, g.X3, "sf1"))
    stages.append(lambda: stage_ln_finalize(g, 1e-5))
    stages.append(lambda: stage_apply(g, g.X3, None, None, g.hbf, None, tag="af1"))
    stages.append(lambda: stage_ffn_a(g, g.in_f1_w1T, g.in_f1_w2T, g.f1_s1c, g.f1_s2c, g.hbf, g.ffh, g.handles["ffh"], "f1a"))
    stages.append(lambda: stage_ffn_mid(g, g.in_f1_h1T, g.f1_sh1c, g.ffh, g.ffg, g.handles["ffg"], "f1m1"))
    stages.append(lambda: stage_ffn_mid(g, g.in_f1_h2T, g.f1_sh2c, g.ffg, g.ffh, g.handles["ffh"], "f1m2"))
    stages.append(lambda: stage_ffn_c(g, g.in_f1_w3T, g.f1_s3c, g.ffh, g.X3, g.out_y, "f1c"))


def _prep_inputs(inputs):
    f = np.ascontiguousarray
    xs = np.asarray(inputs["L0_attn_x"], dtype=np.float32)
    shared = {
        "w_qkv": f(np.asarray(inputs["L0_attn_w_qkv"], np.float32)),
        "b_qkv": f(np.asarray(inputs["L0_attn_b_qkv"], np.float32).reshape(24, 128)),
        "w_out": f(np.asarray(inputs["L0_attn_w_out"], np.float32)),
        "b_out": f(np.asarray(inputs["L0_attn_b_out"], np.float32).reshape(8, 128)),
        "ln_gamma": f(np.asarray(inputs["L0_attn_ln_gamma"], np.float32).reshape(8, 128)),
        "ln_beta": f(np.asarray(inputs["L0_attn_ln_beta"], np.float32).reshape(8, 128)),
    }
    for L, pre in (("f0", "L0_ffn"), ("f1", "L1_ffn")):
        shared[f"{L}_w1T"] = f(np.asarray(inputs[f"{pre}_w1_w"], np.float32).T)
        shared[f"{L}_w2T"] = f(np.asarray(inputs[f"{pre}_w2_w"], np.float32).T)
        shared[f"{L}_h1T"] = f(np.asarray(inputs[f"{pre}_h1_w"], np.float32).T)
        shared[f"{L}_h2T"] = f(np.asarray(inputs[f"{pre}_h2_w"], np.float32).T)
        shared[f"{L}_w3T"] = f(np.asarray(inputs[f"{pre}_w3_w"], np.float32).T)
        shared[f"{L}_s1"] = f(np.asarray(inputs[f"{pre}_w1_s"], np.float32).reshape(32, 128))
        shared[f"{L}_s2"] = f(np.asarray(inputs[f"{pre}_w2_s"], np.float32).reshape(32, 128))
        shared[f"{L}_sh1"] = f(np.asarray(inputs[f"{pre}_h1_s"], np.float32).reshape(32, 128))
        shared[f"{L}_sh2"] = f(np.asarray(inputs[f"{pre}_h2_s"], np.float32).reshape(32, 128))
        shared[f"{L}_s3"] = f(np.asarray(inputs[f"{pre}_w3_s"], np.float32).reshape(8, 128))
    shared["Wq"] = f(np.asarray(inputs["L1_attn_Wq"], np.float32))
    shared["Wk"] = f(np.asarray(inputs["L1_attn_Wk"], np.float32))
    shared["Wv"] = f(np.asarray(inputs["L1_attn_Wv"], np.float32))
    shared["Wo"] = f(np.asarray(inputs["L1_attn_Wo"], np.float32))
    shared["W1"] = f(np.asarray(inputs["L1_attn_W1"], np.float32))
    shared["W2"] = f(np.asarray(inputs["L1_attn_W2"], np.float32))
    for nmv, key, w in (("bq", "L1_attn_bq", 8), ("bk", "L1_attn_bk", 8), ("bv", "L1_attn_bv", 8),
                        ("bo", "L1_attn_bo", 8), ("b1", "L1_attn_b1", 32), ("b2", "L1_attn_b2", 8),
                        ("ln1_s", "L1_attn_ln1_s", 8), ("ln1_b", "L1_attn_ln1_b", 8),
                        ("ln2_s", "L1_attn_ln2_s", 8), ("ln2_b", "L1_attn_ln2_b", 8)):
        shared[nmv] = f(np.asarray(inputs[key], np.float32).reshape(NB, w, 128))
    in_maps = []
    for c in range(8):
        m = dict(shared)
        m["x_fm"] = f(xs[c].T)
        in_maps.append(m)
    return in_maps


def kernel(**inputs):
    in_maps = _prep_inputs(inputs)
    nc, g = build_program()
    res = run_bass_kernel_spmd(nc, in_maps, core_ids=list(range(8)))
    out = np.empty((8, T, D), dtype=np.float32)
    for c in range(8):
        out[c] = res.results[c]["y_fm"].T
    return out
